# version 19
# baseline (speedup 1.0000x reference)
import numpy as np
import ml_dtypes
import concourse.bass as bass
import concourse.mybir as mybir
from concourse.bass_utils import run_bass_kernel_spmd

F32 = mybir.dt.float32
BF16 = mybir.dt.bfloat16
U8 = mybir.dt.uint8
AF = mybir.ActivationFunctionType
ALU = mybir.AluOpType
BF = ml_dtypes.bfloat16

S = 2048
D = 1024
H = 16
DH = 64
NT = 16
NJ = 8
KB = 1024

CUT = None
PARANOID = False


def _hilo(x64):
    hi = x64.astype(np.float32)
    lo = (x64 - hi.astype(np.float64)).astype(np.float32)
    return hi, lo


def make_consts():
    c = {}
    i0 = np.arange(128)
    c["ident"] = np.eye(128, dtype=BF)
    c["triu"] = (i0[:, None] <= i0[None, :]).astype(BF)
    c["tincl"] = (i0[:, None] <= i0[None, :]).astype(np.float32)
    c["msel"] = (i0[:, None] <= i0[None, :]).astype(np.uint8)
    c["mline_d"] = (1.0 - (i0[:, None] == i0[None, :] + 1)).astype(BF)
    ml_s = np.ones((128, 128), dtype=np.float64)
    ml_s[0, 127] = 0.0
    c["mline_s"] = ml_s.astype(BF)
    c["mlineinv"] = (i0[:, None] != i0[None, :] + 1).astype(np.uint8)
    msupl = np.ones((128, 128), dtype=np.uint8)
    msupl[0, 127] = 0
    c["msuplinv"] = msupl
    m = np.arange(32, dtype=np.float64)
    w = 1.0 / 10000.0 ** (2.0 * m / 64.0)
    qpos = np.arange(S, dtype=np.float64)
    ang = w[:, None] * qpos[None, :]
    sinq = np.sin(ang); cosq = np.cos(ang)
    c["sinq_hi"], c["sinq_lo"] = _hilo(np.concatenate([sinq] * 4, 0))
    c["cosq_hi"], c["cosq_lo"] = _hilo(np.concatenate([cosq] * 4, 0))
    phi = np.concatenate([cosq, sinq], 0)
    c["phi_hi"], c["phi_lo"] = _hilo(np.concatenate([phi] * 2, 0))
    cS = np.cos(S * w); sS = np.sin(S * w)
    c["cs_hi"], c["cs_lo"] = _hilo(np.concatenate([cS] * 4)[:, None])
    c["ss_hi"], c["ss_lo"] = _hilo(np.concatenate([-sS, sS, -sS, sS])[:, None])
    c["sgn"] = np.concatenate([np.ones(32), -np.ones(32), np.ones(32), -np.ones(32)]).astype(np.float32)[:, None]
    return c


def _chunk8(vec):
    return np.ascontiguousarray(vec.reshape(8, 128).T)


def bc_last(ap, n):
    return bass.AP(ap.tensor, ap.offset, list(ap.ap) + [[0, n]])


def sq_last(ap):
    a = list(ap.ap)
    assert a[-1][1] == 1, a
    return bass.AP(ap.tensor, ap.offset, a[:-1])


ENGS = ("T", "V", "Sc", "Y", "G")
EKEY = {"T": "tT", "V": "tV", "Sc": "tS", "Y": "dY", "G": "dG"}


class Plan:
    def __init__(self):
        self.ops = {e: [] for e in ENGS}
        self.cnt = {"tT": 0, "tV": 0, "tS": 0, "dY": 0, "dG": 0, "dS": 0}
        self.waited = {e: dict() for e in ENGS}
        self.dirty = {e: False for e in ("T", "V", "Sc")}
        self.sems = None

    def _add(self, eng, semkey, fn, waits, inc, drain=False):
        wl = []
        for sname, val in waits:
            if val > 0 and val > self.waited[eng].get(sname, 0):
                self.waited[eng][sname] = val
                wl.append((sname, val))
        tick = None
        if inc:
            self.cnt[semkey] += 16 if semkey.startswith("d") else 1
            tick = self.cnt[semkey]
            if eng in self.dirty:
                self.dirty[eng] = False
        elif eng in self.dirty:
            self.dirty[eng] = True
        plan = self

        def closure(e, wl=tuple(wl), fn=fn, tick=tick, semkey=semkey, drain=drain):
            for sname, val in wl:
                e.wait_ge(plan.sems[sname], val)
            ins = fn(e)
            if drain:
                ins = e.drain()
            if tick is not None:
                ins.then_inc(plan.sems[semkey], 16 if semkey.startswith("d") else 1)
        self.ops[eng].append(closure)
        return self.cnt[semkey]

    def T(self, fn, waits=(), inc=False):
        return self._add("T", "tT", fn, waits, inc)

    def V(self, fn, waits=(), inc=False, drain=False):
        return self._add("V", "tV", fn, waits, inc, drain)

    def Sc(self, fn, waits=(), inc=False, drain=False):
        return self._add("Sc", "tS", fn, waits, inc, drain)

    def Y(self, fn, waits=()):
        return self._add("Y", "dY", fn, waits, True)

    def G(self, fn, waits=()):
        return self._add("G", "dG", fn, waits, True)

    def Yf(self):
        v = self.cnt["dY"]
        plan = self
        if v > self.waited["Y"].get("dY", 0):
            self.waited["Y"]["dY"] = v
            self.ops["Y"].append(lambda e, v=v: e.wait_ge(plan.sems["dY"], v))
        return v

    def Gf(self):
        v = self.cnt["dG"]
        plan = self
        if v > self.waited["G"].get("dG", 0):
            self.waited["G"]["dG"] = v
            self.ops["G"].append(lambda e, v=v: e.wait_ge(plan.sems["dG"], v))
        return v

    def fence(self, eng):
        if self.dirty.get(eng):
            key = EKEY[eng]
            self.cnt[key] += 1
            tick = self.cnt[key]
            plan = self

            def closure(e, tick=tick, key=key):
                e.nop().then_inc(plan.sems[key], 1)
            self.ops[eng].append(closure)
            self.dirty[eng] = False

    def barrier(self):
        for e in ("T", "V", "Sc"):
            self.fence(e)
        self.Yf()
        self.Gf()
        snap = [(k, v) for k, v in self.cnt.items() if v > 0]
        plan = self
        for e in ENGS:
            for k, v in snap:
                if v > self.waited[e].get(k, 0):
                    self.waited[e][k] = v

            def closure(eng, snap=tuple(snap)):
                for k, v in snap:
                    eng.wait_ge(plan.sems[k], v)
            self.ops[e].append(closure)


def cn(h):
    return 512 * (h // 7) + 65 * (h % 7)


def build_nc(cut=None, dbg=False):
    nc = bass.Bass("TRN2", target_bir_lowering=False, debug=False)

    d = {}
    def din(name, shape, dtype=F32):
        d[name] = nc.dram_tensor(name, list(shape), dtype, kind="ExternalInput").ap()
        return d[name]

    x_d = din("x", (S, D))
    for nm in ("wq_a", "wk_a", "wv_a", "wo_a", "wr_t", "wk_t", "wv_t",
               "wq_b", "wk_b", "wv_b", "wo_b"):
        din(nm, (D, D))
    din("wa_a", (D, H))
    din("ba", (1, H)); din("bo", (1, D))
    din("mu_r8", (128, 8)); din("mu_k8", (128, 8))
    din("u8", (128, 8)); din("v8", (128, 8))
    din("ident", (128, 128), BF16); din("triu", (128, 128), BF16)
    din("tincl", (128, 128), F32); din("msel", (128, 128), U8)
    din("mline_d", (128, 128), BF16); din("mline_s", (128, 128), BF16)
    din("mlineinv", (128, 128), U8); din("msuplinv", (128, 128), U8)
    for nm in ("sinq_hi", "sinq_lo", "cosq_hi", "cosq_lo", "phi_hi", "phi_lo"):
        din(nm, (128, S))
    for nm in ("cs_hi", "cs_lo", "ss_hi", "ss_lo", "sgn"):
        din(nm, (128, 1))
    xspill_d = nc.dram_tensor("xspill", [S, D], F32, kind="Internal").ap()
    out_d = nc.dram_tensor("out", [S, D], F32, kind="ExternalOutput").ap()
    dbg_outs = {}
    def dump(name, sb_ap, dtype):
        if not dbg or name in dbg_outs:
            return
        shape = list(sb_ap.shape)
        dd = nc.dram_tensor("dbg_" + name, shape, dtype, kind="ExternalOutput").ap()
        dbg_outs[name] = dd
        p.barrier()
        tk = p.Y(lambda e, dd=dd, sb_ap=sb_ap: e.dma_start(dd, sb_ap))
        p.barrier()

    def tv(ap):
        return ap.rearrange("(t p) c -> p t c", p=128)
    x_v, xs_v, out_v = tv(x_d), tv(xspill_d), tv(out_d)
    wvw = {nm: d[nm].rearrange("(j p) c -> p j c", p=128)
           for nm in ("wq_a", "wk_a", "wv_a", "wo_a", "wr_t", "wk_t", "wv_t",
                      "wq_b", "wk_b", "wv_b", "wo_b")}
    wa_v = d["wa_a"].rearrange("(j p) c -> p j c", p=128)

    alloc = nc.alloc_sbuf_tensor

    def place(arena, off, name, shape, dtype):
        return nc.alloc_sbuf_tensor_at(
            name, list(shape), dtype, offset=nc.lookup_mloc(arena).addr + off)

    SL1 = alloc("SL1", [128, 32768], U8)
    SL2 = alloc("SL2", [128, 32768], U8)
    SL3 = alloc("SL3", [128, 32768], U8)
    SL4 = alloc("SL4", [128, 33280], U8)
    XAR = alloc("XAR", [128, 32768], U8)
    SCR = alloc("SCR", [128, 10240], U8)
    GB = alloc("GB", [128, 4096], U8)

    XNT = place(SL1, 0, "XNT", [128, NJ, S], BF16)
    KAT = place(SL1, 0, "KAT", [128, NJ, S], BF16)
    XOLD = place(SL1, 0, "XOLD", [128, 4, KB], F32)
    XT0 = place(SL2, 0, "XT0", [128, 8, KB], F32)
    QA = place(SL2, 0, "QA", [128, NT, KB], BF16)
    QUT = place(SL2, 0, "QUT", [128, 4, S], BF16)
    KTB = place(SL2, 16384, "KTB", [128, 4, S], BF16)
    XT1 = place(SL3, 0, "XT1", [128, 8, KB], F32)
    KA = place(SL3, 0, "KA", [128, NT, KB], BF16)
    VPB = place(SL3, 0, "VPB", [128, NT, 8 * 65], BF16)
    UTA = place(SL3, 16640, "UTA", [128, S], BF16)
    V2A = place(SL3, 20736, "V2A", [128, S], BF16)
    VPA = place(SL4, 0, "VPA", [128, NT, H * 65], BF16)
    RG = place(SL4, 0, "RG", [128, NT, KB], BF16)
    SINR = place(SL4, 0, "SINR", [128, S], BF16)
    COSR = place(SL4, 4096, "COSR", [128, S], BF16)
    PHIR = place(SL4, 8192, "PHIR", [128, S], BF16)
    QVT = place(SL4, 12288, "QVT", [128, S], BF16)
    SWP = place(SL4, 16384, "SWP", [128, S], BF16)
    UTB = place(SL4, 20480, "UTB", [128, S], BF16)
    V2B = place(SL4, 24576, "V2B", [128, S], BF16)
    AOT = place(XAR, 0, "AOT", [128, NJ, S], BF16)
    XRB = place(XAR, 0, "XRB", [128, NJ, S], BF16)
    TBSTG = place(XAR, 0, "TBSTG", [128, 2, S], F32)
    TBF2 = TBSTG[:, 1, 0:512]
    TBE2 = TBSTG[:, 1, 512:1024]
    QTb = place(SCR, 0, "QTb", [64, H, 128], BF16)
    KTb = place(SCR, 4096, "KTb", [64, H, 128], BF16)
    AOC = place(SCR, 8192, "AOC", [128, KB], BF16)
    GT0 = place(SCR, 8192, "GT0", [128, H], F32)
    GT1 = place(SCR, 8320, "GT1", [128, H], F32)
    TMPD = place(SCR, 0, "TMPD", [128, S], BF16)
    ATT = place(SCR, 0, "ATT", [128, 4, 512], BF16)
    E2S = place(SCR, 4096, "E2S", [128, 2, 128], BF16)
    E3S = place(SCR, 6656, "E3S", [128, 2, 256], BF16)
    ODDS = place(SCR, 4608, "ODDS", [128, 2, 512], BF16)
    RSCR = place(SCR, 7680, "RSCR", [128, 512], F32)
    SQ = place(GB, 0, "SQ", [128, NT, H], F32)
    SK = place(GB, 1024, "SK", [128, NT, H], F32)
    LAM = place(GB, 2048, "LAM", [128, NT, H], F32)
    AALL = place(GB, 3072, "AALL", [128, NT, H], F32)
    BOBC = place(GB, 0, "BOBC", [128, KB], F32)

    ident = alloc("identb", [128, 128], BF16)
    allob = alloc("allob", [128, 128], BF16)
    triub = alloc("triub", [128, 128], BF16)
    tincl = alloc("tinclb", [128, 128], F32)
    allof = alloc("allof", [128, 128], F32)
    msel = alloc("mselb", [128, 128], U8)
    mlind = alloc("mlind", [128, 128], BF16)
    mlins = alloc("mlins", [128, 128], BF16)
    mlinv = alloc("mlinv", [128, 128], U8)
    msupv = alloc("msupv", [128, 128], U8)
    mur = alloc("mur", [128, 8], F32)
    muk = alloc("muk", [128, 8], F32)
    u8b = alloc("u8b", [128, 8], F32)
    v8b = alloc("v8b", [128, 8], F32)
    vmu = alloc("vmu", [128, 8], F32)
    csv = alloc("csv", [128, 1], F32)
    ssv = alloc("ssv", [128, 1], F32)
    sgn = alloc("sgnb", [128, 1], F32)
    hl0 = alloc("hl0", [128, 1], F32)
    bab = alloc("bab", [128, H], F32)
    wasb = alloc("wasb", [128, NJ, H], BF16)
    WS = [alloc("WS0", [128, NJ, 512], BF16), alloc("WS1", [128, NJ, 512], BF16)]
    SF = alloc("SF", [64, H * 65], F32)
    SBb = alloc("SBb", [64, H * 65], BF16)
    XNS = alloc("XNS", [128, KB], BF16)
    MTS = alloc("MTS", [128, 2, 128], BF16)
    TMPF = alloc("TMPF", [128, 512], F32)
    TMPE = alloc("TMPE", [128, 512], F32)
    MV = alloc("MV", [128, NT, 2], F32)
    MV2 = alloc("MV2", [128, NT, 4], F32)
    ST12 = alloc("ST12", [128, 12], F32)
    RS = alloc("RSb", [128, NT], F32)
    NMS = alloc("NMSb", [128, NT], F32)
    DENS = alloc("DENS", [128, 16], F32)
    RECS = alloc("RECS", [128, 16], F32)
    CEPS = alloc("CEPS", [128, 1], F32)
    nc.const_aps.register(F32, 1e-5, CEPS[:, 0:1])

    psR = [nc.alloc_psum_tensor(f"psR{i}", [128, 512], F32) for i in range(3)]
    psX = [nc.alloc_psum_tensor(f"psX{i}", [128, 1024], BF16) for i in range(2)]
    psN = nc.alloc_psum_tensor("psN", [128, 1536], F32)
    _pb = nc.psum_base
    nc.psum_base = 0
    psRR = nc.alloc_psum_tensor("psRR", [128, 1024], F32)
    nc.psum_base = 3
    psXX = nc.alloc_psum_tensor("psXX", [128, 1024], F32)
    nc.psum_base = 3
    psXA = nc.alloc_psum_tensor("psXA", [128, 512], F32)
    psXB = nc.alloc_psum_tensor("psXB", [128, 512], F32)
    nc.psum_base = _pb

    p = Plan()

    def xtile(t):
        return (XT0 if t < 8 else XT1)[:, t % 8, :]

    def emit_ln_stats(t, waits=()):
        p.V(lambda e, t=t: e.bn_stats(ST12[:, 0:6], xtile(t)[:, 0:512]),
            waits=waits, drain=True)
        p.V(lambda e, t=t: e.bn_stats(ST12[:, 6:12], xtile(t)[:, 512:1024]), drain=True)
        p.V(lambda e, t=t: e.bn_aggr(MV2[:, t, 0:2], ST12[:, 0:6]), drain=True)
        p.V(lambda e, t=t: e.bn_aggr(MV2[:, t, 2:4], ST12[:, 6:12]), inc=True, drain=True)

    xload = []
    xf = {}
    for t in range(NT):
        xload.append(p.Y(lambda e, t=t: e.dma_start(xtile(t), x_v[:, t, :])))
        if t % 4 == 3:
            xf[t // 4] = p.Yf()
    for dst, src in ((ident, "ident"), (triub, "triu"), (tincl, "tincl"),
                     (msel, "msel"), (mlind, "mline_d"), (mlins, "mline_s"),
                     (mlinv, "mlineinv"), (msupv, "msuplinv"),
                     (mur, "mu_r8"), (muk, "mu_k8"), (u8b, "u8"), (v8b, "v8"),
                     (sgn, "sgn")):
        p.Y(lambda e, dst=dst, src=src: e.dma_start(dst[:], d[src]))
    p.Y(lambda e: e.dma_start(csv[:], d["cs_hi"]))
    p.Y(lambda e: e.dma_start(hl0[:], d["cs_lo"]))
    t1 = p.Yf()
    v1 = p.V(lambda e: e.tensor_add(csv[:], csv[:], hl0[:]), waits=[("dY", t1)], inc=True, drain=True)
    p.Y(lambda e: e.dma_start(ssv[:], d["ss_hi"]), waits=[("tV", v1)])
    p.Y(lambda e: e.dma_start(hl0[:], d["ss_lo"]))
    t1 = p.Yf()
    p.V(lambda e: e.tensor_add(ssv[:], ssv[:], hl0[:]), waits=[("dY", t1)], inc=True, drain=True)
    p.V(lambda e: e.tensor_sub(vmu[:], v8b[:], u8b[:]),
        waits=[("dY", t1)], inc=True, drain=True)
    for j in range(NJ):
        p.G(lambda e, j=j: e.dma_start(wasb[:, j, :], wa_v[:, j, :]))
    p.V(lambda e: e.memset(allob[:], 1.0), inc=True)
    p.V(lambda e: e.memset(allof[:], 1.0), inc=True)
    p.V(lambda e: e.memset(CEPS[:], 1e-5), inc=True, drain=True)
    p.G(lambda e: e.dma_start(XNS[0:1, 0:H], d["ba"]))
    tba = p.Gf()
    tmm = p.T(lambda e: e.matmul(psR[2][:, 0:H], allob[0:1, :], XNS[0:1, 0:H],
                                 start=True, stop=True),
              waits=[("dG", tba), ("tV", p.cnt["tV"])], inc=True)
    p.V(lambda e: e.tensor_scalar(bab[:], psR[2][:, 0:H], 2.0, None, op0=ALU.add),
        waits=[("tT", tmm)], inc=True, drain=True)
    for t in range(NT):
        emit_ln_stats(t, waits=[("dY", xf[t // 4])] if t % 4 == 0 else ())
    p.barrier()

    def wslab(slot, wview, half, waits=()):
        for j in range(NJ):
            p.G(lambda e, j=j, s=slot, w=wview, h=half:
                e.dma_start(WS[s][:, j, :], w[:, j, 512 * h:512 * h + 512]),
                waits=waits if j == 0 else ())
        return p.Gf()

    def emit_ln(end_barrier=True, skip_stats=False):
        if not skip_stats:
            for t in range(NT):
                emit_ln_stats(t)
        m1 = sq_last(MV2[:, :, 0:1])
        v1 = sq_last(MV2[:, :, 1:2])
        m2 = sq_last(MV2[:, :, 2:3])
        v2 = sq_last(MV2[:, :, 3:4])
        mean_v = sq_last(MV[:, :, 0:1])
        var_v = sq_last(MV[:, :, 1:2])
        A_, B_, C_, D_ = RS[:], NMS[:], DENS[:], RECS[:]
        p.V(lambda e: e.tensor_add(A_, m1, m2), drain=True)
        p.V(lambda e: e.tensor_sub(B_, m1, m2), drain=True)
        p.V(lambda e: e.tensor_add(D_, v1, v2), drain=True)
        p.V(lambda e: e.tensor_scalar_mul(mean_v, A_, 0.5), drain=True)
        p.V(lambda e: e.tensor_mul(C_, B_, B_), drain=True)
        p.V(lambda e: e.tensor_scalar_mul(A_, D_, 0.5), drain=True)
        p.V(lambda e: e.tensor_scalar_mul(B_, C_, 0.25), drain=True)
        p.V(lambda e: e.tensor_add(var_v, A_, B_), inc=True, drain=True)
        vstats = p.cnt["tV"]
        p.Sc(lambda e: e.activation(NMS[:], var_v, AF.Ln, bias=1e-5),
             waits=[("tV", vstats)], drain=True)
        srs = p.Sc(lambda e: e.activation(RS[:], NMS[:], AF.Exp, scale=-0.5),
                   inc=True, drain=True)
        copy_t = {}
        trans_t = {}
        for t in range(NT):
            w = [("tS", srs)]
            if t > 0:
                w.append(("tT", trans_t[t - 1]))
            p.V(lambda e, t=t: e.tensor_scalar(
                XNS[:], xtile(t), MV[:, t, 0:1], RS[:, t:t + 1],
                op0=ALU.subtract, op1=ALU.mult), waits=w, inc=True)
            vxn = p.cnt["tV"]
            bank = psX[t % 2]
            wb = [("tV", vxn)]
            if t >= 2:
                wb.append(copy_t[t - 2])
            for j in range(NJ):
                p.T(lambda e, j=j, bank=bank: e.transpose(
                    bank[:, 128 * j:128 * j + 128], XNS[:, 128 * j:128 * j + 128],
                    ident[:]), waits=wb if j == 0 else (), inc=(j == NJ - 1))
            trans_t[t] = p.cnt["tT"]
            p.Sc(lambda e, t=t, bank=bank: e.copy(
                XNT[:, :, 128 * t:128 * t + 128],
                bank[:].rearrange("p (j c) -> p j c", c=128)),
                waits=[("tT", trans_t[t])], inc=True, drain=True)
            copy_t[t] = ("tS", p.cnt["tS"])
        dump("XNT", XNT[:], BF16)
        dump("MV", MV[:], F32)
        dump("MV2", MV2[:], F32)
        dump("RS", RS[:], F32)
        if end_barrier:
            p.barrier()
        return copy_t

    def emit_attn_a():
        pre_slab = {0: wslab(0, wvw["wq_a"], 0), 1: wslab(1, wvw["wq_a"], 1)}
        vprev = {0: 0, 1: 0}
        sprev = {0: 0, 1: 0}
        for t in range(NT):
            gpb = psR[2][:, 0:H] if t % 2 == 0 else psN[:, 0:H]
            gt = GT0 if t % 2 == 0 else GT1
            for j in range(NJ):
                p.T(lambda e, t=t, j=j, gpb=gpb: e.matmul(
                    gpb, XNT[:, j, 128 * t:128 * t + 128], wasb[:, j, :],
                    start=(j == 0), stop=(j == NJ - 1)),
                    waits=([("tV", vprev[t % 2])] if j == 0 else ()), inc=(j == NJ - 1))
            tmm = p.cnt["tT"]
            vprev[t % 2] = p.V(lambda e, gt=gt, gpb=gpb: e.tensor_add(
                gt[:], gpb, bab[:]),
                waits=[("tT", tmm), ("tS", sprev[t % 2])], inc=True, drain=True)
            sprev[t % 2] = p.Sc(lambda e, t=t, gt=gt: e.activation(
                AALL[:, t, :], gt[:], AF.Sigmoid),
                waits=[("tV", vprev[t % 2])], inc=True, drain=True)
        aall_flat = AALL[:, :, :].rearrange("p t h -> p (t h)")
        sln = p.Sc(lambda e: e.activation(aall_flat, aall_flat, AF.Ln), inc=True, drain=True)
        sprev = sln
        for t in range(NT):
            tcs = p.T(lambda e, t=t: e.matmul(psR[2][:, 16:32], tincl[:], AALL[:, t, :],
                                              start=True, stop=True),
                      waits=[("tS", sprev)])
            tcs = p.T(lambda e, t=t: e.matmul(psR[2][:, 32:48], allof[:], AALL[:, t, :],
                                              start=False, stop=True), inc=True)
            p.Sc(lambda e, t=t: e.activation(SQ[:, t, :], psR[2][:, 16:32], AF.Exp),
                 waits=[("tT", tcs)], drain=True)
            p.Sc(lambda e, t=t: e.activation(SK[:, t, :], psR[2][:, 16:32], AF.Exp, scale=-1.0),
                 drain=True)
            sprev = p.Sc(lambda e, t=t: e.activation(LAM[:, t, :], psR[2][:, 32:48], AF.Exp),
                         inc=True, drain=True)
        dump("SQ", SQ[:], F32); dump("SK", SK[:], F32)
        dump("LAM", LAM[:], F32); dump("AALL", AALL[:], F32)
        p.barrier()

        p.V(lambda e: e.memset(VPA[:], 1.0), inc=True)

        slab_seq = [("wq_a", 0), ("wq_a", 1), ("wk_a", 0), ("wk_a", 1),
                    ("wv_a", 0), ("wv_a", 1)]
        slab_tick = dict(pre_slab)
        evac_tick = {}
        elu_prev = [0]
        for i, (wnm, half) in enumerate(slab_seq):
            slot = i % 2
            for t in range(NT):
                pb = psR[t % 2]
                w = [("dG", slab_tick[i])]
                if (i, t - 2) in evac_tick:
                    w.append(evac_tick[(i, t - 2)])
                elif t == 0 and (i - 1, NT - 2) in evac_tick:
                    w.append(evac_tick[(i - 1, NT - 2)])
                elif t == 1 and (i - 1, NT - 1) in evac_tick:
                    w.append(evac_tick[(i - 1, NT - 1)])
                for j in range(NJ):
                    p.T(lambda e, t=t, j=j, pb=pb, slot=slot: e.matmul(
                        pb[:], XNT[:, j, 128 * t:128 * t + 128], WS[slot][:, j, :],
                        start=(j == 0), stop=(j == NJ - 1)),
                        waits=w if j == 0 else (), inc=(j == NJ - 1))
                tmm = p.cnt["tT"]
                if wnm in ("wq_a", "wk_a"):
                    bF = TMPF if t % 2 == 0 else TBF2
                    bE = TMPE if t % 2 == 0 else TBE2
                    p.Sc(lambda e, pb=pb, bF=bF: e.activation(
                        bF[:], pb[:], AF.Relu, scale=-1.0),
                        waits=w + [("tT", tmm)], inc=True)
                    p.Sc(lambda e, bF=bF, bE=bE: e.activation(
                        bE[:], bF[:], AF.Exp, scale=-1.0), inc=True)
                    sexp = p.cnt["tS"]
                    p.V(lambda e, pb=pb, bF=bF, bE=bE: e.scalar_tensor_tensor(
                        bF[:], pb[:], 0.0, bE[:], op0=ALU.max, op1=ALU.add),
                        waits=[("tS", sexp)], inc=True)
                    dstb = QA if wnm == "wq_a" else KA
                    scale = SQ if wnm == "wq_a" else SK
                    dv = dstb[:, t, 512 * half:512 * half + 512].rearrange(
                        "p (h c) -> p h c", c=64)
                    p.V(lambda e, dv=dv, t=t, scale=scale, half=half, bF=bF:
                        e.tensor_tensor(
                            dv, bF[:].rearrange("p (h c) -> p h c", c=64),
                            bc_last(scale[:, t, 8 * half:8 * half + 8], 64),
                            op=ALU.mult), inc=True, drain=True)
                    evac_tick[(i, t)] = ("tV", p.cnt["tV"])
                else:
                    vpa_dst = VPA[:, t, 65 * 8 * half:65 * 8 * half + 65 * 8]
                    vpa_dst = vpa_dst.rearrange("p (h c) -> p h c", c=65)[:, :, 0:64]
                    src = pb[:].rearrange("p (h c) -> p h c", c=64)
                    p.Sc(lambda e, vpa_dst=vpa_dst, src=src: e.copy(vpa_dst, src),
                         waits=[("tT", tmm)], inc=True, drain=True)
                    evac_tick[(i, t)] = ("tS", p.cnt["tS"])
            if i + 2 < len(slab_seq):
                wnm2, half2 = slab_seq[i + 2]
                slab_tick[i + 2] = wslab((i + 2) % 2, wvw[wnm2], half2,
                                         waits=[("tT", p.cnt["tT"])])
        dump("QA", QA[:], BF16); dump("KA", KA[:], BF16); dump("VPA", VPA[:], BF16)
        p.barrier()
        wslab(0, wvw["wo_a"], 0)
        wo_st1 = wslab(1, wvw["wo_a"], 1)

        p.V(lambda e: e.memset(SF[:], 0.0), inc=True, drain=True)
        p.V(lambda e: e.memset(SBb[:], 0.0), inc=True, drain=True)
        vS = ("tV", p.cnt["tV"])
        vO = {}
        vSu = ("tV", p.cnt["tV"])
        last_ao_copy = ("tV", 0)
        last_aot_tick = 0
        groups = [(0, 0, 7), (1, 7, 7), (2, 14, 2)]

        def emit_tail(c, gc_wait=None):
            nonlocal vS, vSu, last_ao_copy, last_aot_tick
            for h in range(H):
                bank_first = h in (0, 7, 14)
                p.T(lambda e, h=h, c=c, bf=bank_first: e.matmul(
                    psN[0:64, cn(h):cn(h) + 65], KA[:, c, 64 * h:64 * h + 64],
                    VPA[:, c, 65 * h:65 * h + 65], start=bf, stop=True),
                    waits=([("tV", vO[c])] if h == 0 else ()), inc=(h == H - 1))
            tS = p.cnt["tT"]
            for g, off, cnt_ in groups:
                view = psN[0:64, 512 * g:512 * g + 65 * cnt_].rearrange("p (h c) -> p h c", c=65)
                sfv = SF[:, 65 * off:65 * (off + cnt_)].rearrange("p (h c) -> p h c", c=65)
                p.V(lambda e, view=view, sfv=sfv: e.tensor_tensor(sfv, sfv, view, op=ALU.add),
                    waits=[("tT", tS)] if g == 0 else ())
                p.V(lambda e, sfv=sfv, off=off, cnt_=cnt_, c=c: e.tensor_tensor(
                    sfv, sfv, bc_last(LAM[0:64, c, off:off + cnt_], 65), op=ALU.mult),
                    inc=(g == 2), drain=(g == 2))
            vsf = p.cnt["tV"]
            p.Sc(lambda e: e.copy(SBb[:], SF[:]), waits=[("tV", vsf)],
                 inc=True, drain=True)
            vSu = ("tS", p.cnt["tS"])
            vS = vSu
            bank = psX[c % 2]
            for j in range(NJ):
                w0 = [("tV", vO[c]), last_ao_copy]
                if gc_wait is not None:
                    w0.append(gc_wait)
                p.T(lambda e, j=j, bank=bank: e.transpose(
                    bank[:, 128 * j:128 * j + 128], AOC[:, 128 * j:128 * j + 128],
                    ident[:]),
                    waits=w0 if j == 0 else (),
                    inc=(j == NJ - 1))
            last_aot_tick = p.cnt["tT"]
            p.Sc(lambda e, c=c, bank=bank: e.copy(
                AOT[:, :, 128 * c:128 * c + 128],
                bank[:].rearrange("p (j c) -> p j c", c=128)),
                waits=[("tT", last_aot_tick)], inc=True, drain=True)
            last_ao_copy = ("tS", p.cnt["tS"])

        for c in range(NT):
            gsrc = [(QA, QTb, 0), (QA, QTb, 1), (KA, KTb, 0), (KA, KTb, 1)]
            gcopy = {}
            for g, (srcb, dstb, hi) in enumerate(gsrc):
                bank = psX[g % 2]
                w = []
                if g >= 2:
                    w.append(gcopy[g - 2])
                elif c > 0:
                    w.append(last_ao_copy)
                for hh in range(8):
                    hg = 8 * hi + hh
                    p.T(lambda e, hh=hh, hg=hg, c=c, bank=bank, srcb=srcb: e.transpose(
                        bank[0:64, 128 * hh:128 * hh + 128],
                        srcb[:, c, 64 * hg:64 * hg + 64], ident[:]),
                        waits=w if hh == 0 else (), inc=(hh == 7))
                ttr = p.cnt["tT"]
                p.Sc(lambda e, dstb=dstb, hi=hi, bank=bank: e.copy(
                    dstb[:, 8 * hi:8 * hi + 8, :],
                    bank[0:64, :].rearrange("p (h c) -> p h c", c=128)),
                    waits=[("tT", ttr)], inc=True, drain=True)
                gcopy[g] = ("tS", p.cnt["tS"])
            if c > 0:
                emit_tail(c - 1, gc_wait=gcopy[2 + ((c - 1) % 2)])
            vmask = {}
            tnum = 0

            def emit_mt(h):
                pb = psR[h % 2]
                if h >= 2:
                    w = [("tV", vmask[h - 2])]
                else:
                    w = [gcopy[2] if h < 8 else gcopy[3]]
                tmt = p.T(lambda e, h=h, pb=pb: e.matmul(
                    pb[:, 0:128], KTb[:, h, :], QTb[:, h, :], start=True, stop=True),
                    waits=w + [gcopy[3] if h >= 8 else gcopy[2]], inc=True)
                vmask[h] = p.V(lambda e, h=h, pb=pb: e.tensor_tensor(
                    MTS[:, h % 2, :], pb[:, 0:128], triub[:], op=ALU.mult),
                    waits=[("tT", tmt)], inc=True, drain=True)

            def emit_num(h):
                pb = psR[h % 2]
                bank_first = h in (0, 7, 14)
                w2 = [("tV", vmask[h]), vSu, vS]
                p.T(lambda e, h=h, c=c, bf=bank_first: e.matmul(
                    psN[:, cn(h):cn(h) + 65], MTS[:, h % 2, :],
                    VPA[:, c, 65 * h:65 * h + 65], start=bf, stop=False),
                    waits=w2)
                p.T(lambda e, h=h: e.matmul(
                    psN[:, cn(h):cn(h) + 65], QTb[:, h, :], SBb[:, 65 * h:65 * h + 65],
                    start=False, stop=True), inc=(h == H - 1))

            for h in range(H):
                emit_mt(h)
                if h >= 1:
                    emit_num(h - 1)
            emit_num(H - 1)
            tnum = p.cnt["tT"]
            for g, off, cnt_ in groups:
                view = psN[:, 512 * g:512 * g + 65 * cnt_].rearrange("p (h c) -> p h c", c=65)
                den = sq_last(view[:, :, 64:65])
                p.V(lambda e, den=den, off=off, cnt_=cnt_: e.tensor_scalar(
                    DENS[:, off:off + cnt_], den, 1e-6, None, op0=ALU.add),
                    waits=[("tT", tnum)], drain=True)
            p.V(lambda e: e.reciprocal(RECS[:], DENS[:]), inc=True, drain=True)
            vrec = p.cnt["tV"]
            aoc_v = AOC[:].rearrange("p (h c) -> p h c", c=64)
            for g, off, cnt_ in groups:
                view = psN[:, 512 * g:512 * g + 65 * cnt_].rearrange("p (h c) -> p h c", c=65)
                p.V(lambda e, view=view, off=off, cnt_=cnt_, aoc_v=aoc_v: e.tensor_tensor(
                    aoc_v[:, off:off + cnt_, :], view[:, :, 0:64],
                    bc_last(RECS[:, off:off + cnt_], 64), op=ALU.mult),
                    waits=[("tV", vrec), ("tT", last_aot_tick)],
                    inc=(g == 2), drain=(g == 2))
            vO[c] = p.cnt["tV"]
        emit_tail(NT - 1)
        p.barrier()

        emit_wo(wvw["wo_a"], x_v, add_bo=False, st1=wo_st1)

    def emit_wo(wo_view, xold_view, add_bo, st1=None):
        if st1 is None:
            wslab(0, wo_view, 0)
            st1 = wslab(1, wo_view, 1)
        ld_tick = {}
        ev_tick = {}
        for t in range(4):
            p.Y(lambda e, t=t: e.dma_start(XOLD[:, t % 4, :], xold_view[:, t, :]))
            ld_tick[t] = p.Yf()
        for t in range(NT):
            for half in range(2):
                pb = psR[(2 * t + half) % 2]
                w = [("dG", st1)]
                k_prev = (t, half - 2) if half >= 2 else (t - 1, half)
                if k_prev in ev_tick:
                    w.append(("tV", ev_tick[k_prev]))
                for j in range(NJ):
                    p.T(lambda e, t=t, j=j, pb=pb, half=half: e.matmul(
                        pb[:], AOT[:, j, 128 * t:128 * t + 128], WS[half][:, j, :],
                        start=(j == 0), stop=(j == NJ - 1)),
                        waits=w if j == 0 else (), inc=(j == NJ - 1))
                tmm = p.cnt["tT"]
                wv_ = [("tT", tmm), ("dY", ld_tick[t])]
                if add_bo:
                    p.V(lambda e, half=half, pb=pb: e.tensor_tensor(
                        TMPF[:], pb[:], BOBC[:, 512 * half:512 * half + 512], op=ALU.add),
                        waits=wv_, inc=True)
                    p.V(lambda e, t=t, half=half: e.tensor_tensor(
                        xtile(t)[:, 512 * half:512 * half + 512], TMPF[:],
                        XOLD[:, t % 4, 512 * half:512 * half + 512],
                        op=ALU.add), inc=True)
                else:
                    p.V(lambda e, t=t, half=half, pb=pb: e.tensor_tensor(
                        xtile(t)[:, 512 * half:512 * half + 512], pb[:],
                        XOLD[:, t % 4, 512 * half:512 * half + 512], op=ALU.add),
                        waits=wv_, inc=True)
                ev_tick[(t, half)] = p.cnt["tV"]
                if half == 1:
                    emit_ln_stats(t)
            if t + 4 < NT:
                p.Y(lambda e, t=t: e.dma_start(XOLD[:, t % 4, :], xold_view[:, t + 4, :]),
                    waits=[("tV", ev_tick[(t, 1)])])
                ld_tick[t + 4] = p.Yf()
        p.barrier()

    spill_ticks = [0, 0]

    def emit_ffn(spill_x=False):
        wslab(0, wvw["wr_t"], 0)
        st_r = wslab(1, wvw["wr_t"], 1)
        emit_ln(skip_stats=True)
        def emit_mix(mubuf):
            for j in range(NJ):
                p.V(lambda e, j=j: e.tensor_sub(TMPD[:, 1:S], XNT[:, j, 1:S],
                                                XNT[:, j, 0:S - 1]))
                p.V(lambda e, j=j: e.tensor_copy(TMPD[:, 0:1], XNT[:, j, 0:1]))
                p.V(lambda e, j=j, mubuf=mubuf: e.scalar_tensor_tensor(
                    XRB[:, j, 1:S], TMPD[:, 1:S], mubuf[:, j:j + 1],
                    XNT[:, j, 0:S - 1], op0=ALU.mult, op1=ALU.add))
                p.V(lambda e, j=j, mubuf=mubuf: e.tensor_scalar_mul(
                    XRB[:, j, 0:1], XNT[:, j, 0:1], mubuf[:, j:j + 1]),
                    inc=(j == NJ - 1), drain=(j == NJ - 1))
            return p.cnt["tV"]

        vmix = emit_mix(mur)
        st = st_r
        srg = {}
        stk = {}
        for half in range(2):
            for t in range(NT):
                pb = psR[t % 2]
                idx = NT * half + t
                w = [("dG", st), ("tV", vmix)]
                if idx >= 2 and (idx - 2) in srg:
                    w.append(("tS", srg[idx - 2]))
                for j in range(NJ):
                    p.T(lambda e, t=t, j=j, pb=pb, half=half: e.matmul(
                        pb[:], XRB[:, j, 128 * t:128 * t + 128], WS[half][:, j, :],
                        start=(j == 0), stop=(j == NJ - 1)),
                        waits=w if j == 0 else (), inc=(j == NJ - 1))
                tmm = p.cnt["tT"]
                srg[idx] = p.Sc(lambda e, t=t, half=half, pb=pb: e.activation(
                    RG[:, t, 512 * half:512 * half + 512], pb[:], AF.Sigmoid),
                    waits=[("tT", tmm)], inc=True)
            stk[half] = wslab(half, wvw["wk_t"], half, waits=[("tT", p.cnt["tT"])])
        p.barrier()
        vmix = emit_mix(muk)
        vev = {}
        stv = {}
        for half in range(2):
            for jo in range(4):
                jog = 4 * half + jo
                for sgrp in range(4):
                    pb = psR[(4 * jo + sgrp) % 2]
                    w = [("dG", stk[half]), ("tV", vmix)]
                    kprev = (jog, sgrp)
                    idx = 4 * jog + sgrp
                    if idx >= 2 and (idx - 2) in vev:
                        w.append(("tV", vev[idx - 2]))
                    for j in range(NJ):
                        p.T(lambda e, j=j, jo=jo, sgrp=sgrp, pb=pb, half=half: e.matmul(
                            pb[:], WS[half][:, j, 128 * jo:128 * jo + 128],
                            XRB[:, j, 512 * sgrp:512 * sgrp + 512],
                            start=(j == 0), stop=(j == NJ - 1)),
                            waits=w if j == 0 else (), inc=(j == NJ - 1))
                    tmm = p.cnt["tT"]
                    bK = TMPF if idx % 2 == 0 else TMPE
                    p.Sc(lambda e, pb=pb, bK=bK: e.activation(bK[:], pb[:], AF.Relu),
                         waits=w + [("tT", tmm)], inc=True)
                    srelu = p.cnt["tS"]
                    p.V(lambda e, jog=jog, sgrp=sgrp, bK=bK: e.tensor_tensor(
                        KAT[:, jog, 512 * sgrp:512 * sgrp + 512], bK[:], bK[:],
                        op=ALU.mult), waits=[("tS", srelu)], inc=True)
                    vev[idx] = p.cnt["tV"]
            stv[half] = wslab(half, wvw["wv_t"], half, waits=[("tT", p.cnt["tT"])])
        p.barrier()
        ev = {}
        for half in range(2):
            for t in range(NT):
                pb = psR[t % 2]
                w = [("dG", stv[half])]
                idx = NT * half + t
                if idx >= 2 and (idx - 2) in ev:
                    w.append(("tV", ev[idx - 2]))
                for j in range(NJ):
                    p.T(lambda e, t=t, j=j, pb=pb, half=half: e.matmul(
                        pb[:], KAT[:, j, 128 * t:128 * t + 128], WS[half][:, j, :],
                        start=(j == 0), stop=(j == NJ - 1)),
                        waits=w if j == 0 else (), inc=(j == NJ - 1))
                tmm = p.cnt["tT"]
                p.V(lambda e, t=t, half=half, pb=pb: e.tensor_tensor(
                    TMPF[:], pb[:], RG[:, t, 512 * half:512 * half + 512], op=ALU.mult),
                    waits=[("tT", tmm)], inc=True)
                p.V(lambda e, t=t, half=half: e.tensor_tensor(
                    xtile(t)[:, 512 * half:512 * half + 512],
                    xtile(t)[:, 512 * half:512 * half + 512], TMPF[:], op=ALU.add),
                    inc=True, drain=(spill_x and half == 1))
                ev[idx] = p.cnt["tV"]
                if spill_x and half == 1:
                    p.Y(lambda e, t=t: e.dma_start(xs_v[:, t, :], xtile(t)),
                        waits=[("tV", ev[idx])])
                    emit_ln_stats(t)
                    if t == 7:
                        spill_ticks[0] = p.Yf()
                    elif t == NT - 1:
                        spill_ticks[1] = p.Yf()
        p.barrier()

    def emit_attn_b():
        first_slabs = (wslab(0, wvw["wq_b"], 0), wslab(1, wvw["wk_b"], 0))
        ln_copy = emit_ln(end_barrier=False, skip_stats=True)
        tbl_t = 0
        for dst, hi, lo in ((SINR, "sinq_hi", "sinq_lo"), (COSR, "cosq_hi", "cosq_lo"),
                            (PHIR, "phi_hi", "phi_lo")):
            p.G(lambda e, hi=hi: e.dma_start(TBSTG[:, 0, :], d[hi]),
                waits=[("tV", tbl_t)])
            p.G(lambda e, lo=lo: e.dma_start(TBSTG[:, 1, :], d[lo]))
            t1 = p.Gf()
            p.V(lambda e: e.tensor_add(TBSTG[:, 0, :], TBSTG[:, 0, :], TBSTG[:, 1, :]),
                waits=[("dG", t1)], inc=True)
            tbl_t = p.V(lambda e, dst=dst: e.tensor_copy(dst[:], TBSTG[:, 0, :]), inc=True)
        tbl_done = ("tV", tbl_t)
        spill07, spill = spill_ticks[0], spill_ticks[1]

        last_sfix = [0]
        proj_slabs = {0: None, 1: None}
        wo_st1 = [None]
        odd_slot_free = {0: 0, 1: 0}
        proj_slabs[0] = first_slabs
        av_free = {0: [('tV', 0)], 1: [('tV', 0)]}
        bank_last_exp = {0: 0, 1: 0}
        last_e3 = [0]
        last_tb = [0]
        rec_done = [0]
        evac_done = {0: 0, 1: 0}
        pending_tail = [None]
        for half in range(2):
            p.V(lambda e: e.memset(VPB[:], 1.0), waits=[("dY", spill)], inc=True)
            stq, stkk = proj_slabs[half]
            pev = {}
            pidx = 0
            for which in range(2):
                stw = stq if which == 0 else stkk
                wsl = which
                for jc in range(4):
                    jcg = 4 * half + jc
                    for qq in range(4):
                        pb = psR[pidx % 2]
                        w = [("dG", stw), ("dY", spill07)]
                        if half == 0:
                            w.append(ln_copy[4 * qq + 3])
                        if pidx >= 2:
                            w.append(("tV", pev[pidx - 2]))
                        for j in range(NJ):
                            p.T(lambda e, j=j, jc=jc, qq=qq, pb=pb, wsl=wsl: e.matmul(
                                pb[:], WS[wsl][:, j, 128 * jc:128 * jc + 128],
                                XNT[:, j, 512 * qq:512 * qq + 512],
                                start=(j == 0), stop=(j == NJ - 1)),
                                waits=w if j == 0 else (), inc=(j == NJ - 1))
                        tmm = p.cnt["tT"]
                        if which == 0:
                            p.V(lambda e, jc=jc, qq=qq, jcg=jcg, pb=pb: e.tensor_scalar_add(
                                QUT[:, jc, 512 * qq:512 * qq + 512], pb[:],
                                u8b[:, jcg:jcg + 1]), waits=[("tT", tmm)], inc=True)
                        else:
                            p.V(lambda e, jc=jc, qq=qq, pb=pb: e.tensor_copy(
                                KTB[:, jc, 512 * qq:512 * qq + 512], pb[:]),
                                waits=[("tT", tmm)], inc=True)
                        pev[pidx] = p.cnt["tV"]
                        pidx += 1
                if which == 0:
                    stvv = wslab(0, wvw["wv_b"], half, waits=[("tT", p.cnt["tT"])])
            for t in range(NT):
                pb = psR[t % 2]
                w = [("dG", stvv)]
                if t >= 2:
                    w.append(("tV", pev[("v", t - 2)]))
                else:
                    w.append(("tV", pev[pidx - 1]))
                for j in range(NJ):
                    p.T(lambda e, t=t, j=j, pb=pb: e.matmul(
                        pb[:], XNT[:, j, 128 * t:128 * t + 128], WS[0][:, j, :],
                        start=(j == 0), stop=(j == NJ - 1)),
                        waits=w if j == 0 else (), inc=(j == NJ - 1))
                tmm = p.cnt["tT"]
                vpb_dst = VPB[:, t, :].rearrange("p (h c) -> p h c", c=65)[:, :, 0:64]
                src = pb[:].rearrange("p (h c) -> p h c", c=64)
                p.V(lambda e, vpb_dst=vpb_dst, src=src: e.tensor_copy(vpb_dst, src),
                    waits=[("tT", tmm)], inc=True, drain=True)
                pev[("v", t)] = p.cnt["tV"]
            if half == 0:
                proj_slabs[1] = (wslab(0, wvw["wq_b"], 1, waits=[("tT", p.cnt["tT"])]),
                                 wslab(1, wvw["wk_b"], 1, waits=[("tT", p.cnt["tT"])]))
            else:
                wslab(0, wvw["wo_b"], 0, waits=[("tT", p.cnt["tT"])])
                wo_st1[0] = wslab(1, wvw["wo_b"], 1)
            p.barrier()
            if half == 0:
                p.V(lambda e: e.memset(V2A[:, S - 1:S], 0.0))
                p.V(lambda e: e.memset(V2B[:, S - 1:S], 0.0))

            for jc in range(4):
                jcg = 4 * half + jc
                Ub, V2b = (UTA, V2A) if jc % 2 == 0 else (UTB, V2B)
                vq = p.V(lambda e, jc=jc, jcg=jcg: e.tensor_scalar_add(
                    QVT[:], QUT[:, jc, :], vmu[:, jcg:jcg + 1]), inc=True)
                for blki, (dst0, src0) in enumerate(((0, 32), (32, 0), (64, 96), (96, 64))):
                    p.Y(lambda e, dst0=dst0, src0=src0: e.dma_start(
                        SWP[dst0:dst0 + 32, :], QVT[src0:src0 + 32, :]),
                        waits=[("tV", vq)] if blki == 0 else ())
                sw = p.Yf()
                p.V(lambda e, Ub=Ub: e.tensor_tensor(Ub[:], QVT[:], SINR[:], op=ALU.mult),
                    waits=[("tV", vq)] + ([tbl_done] if half == 0 and jc == 0 else []))
                p.V(lambda e: e.scalar_tensor_tensor(QVT[:], SWP[:], sgn[:], COSR[:],
                                                     op0=ALU.mult, op1=ALU.mult),
                    waits=[("dY", sw)])
                vu = p.V(lambda e, Ub=Ub: e.tensor_tensor(Ub[:], Ub[:], QVT[:], op=ALU.add),
                         inc=True)
                for blki, (dst0, src0) in enumerate(((0, 32), (32, 0), (64, 96), (96, 64))):
                    p.Y(lambda e, dst0=dst0, src0=src0, Ub=Ub: e.dma_start(
                        SWP[dst0:dst0 + 32, :], Ub[src0:src0 + 32, :]),
                        waits=[("tV", vu)] if blki == 0 else ())
                sw = p.Yf()
                p.V(lambda e, Ub=Ub, V2b=V2b: e.tensor_scalar_mul(
                    V2b[:, 0:S - 1], Ub[:, 1:S], csv[:]), waits=[("tV", vu)])
                p.V(lambda e: e.tensor_scalar_mul(QVT[:, 0:S - 1], SWP[:, 1:S], ssv[:]),
                    waits=[("dY", sw)])
                vu2 = p.V(lambda e, V2b=V2b: e.tensor_add(
                    V2b[:, 0:S - 1], V2b[:, 0:S - 1], QVT[:, 0:S - 1]),
                    inc=True, drain=True)

                for qc in range(4):
                    exp_t = {}
                    av_t = {}
                    fix_v = {}
                    e3_t = {}
                    sfix_t = {}
                    d2_mm_t = {}
                    d3_mm_t = {}

                    def emit_scores(kc, e2):
                        po = 64 * e2
                        bidx = kc % 2
                        pbt = (psR[0], psR[1], psXA, psXB)[2 * bidx + e2]
                        pb = pbt[:]
                        lo_, hi_ = 4 * qc, 4 * qc + 4
                        band = lo_ <= kc <= hi_
                        w = [("tV", vu2), ("tS", bank_last_exp[bidx])]
                        p.T(lambda e, kc=kc, pb=pb, po=po, jc=jc, qc=qc: e.matmul(
                            pb, KTB[po:po + 64, jc, 128 * kc:128 * kc + 128],
                            QUT[po:po + 64, jc, 512 * qc:512 * qc + 512],
                            start=True, stop=False), waits=w)
                        sub_d = kc - 4 * qc if band else -1
                        sub_s = sub_d - 1
                        if not band:
                            rel = Ub if kc < lo_ else V2b
                            p.T(lambda e, kc=kc, pb=pb, po=po, rel=rel, qc=qc: e.matmul(
                                pb, PHIR[po:po + 64, 128 * kc:128 * kc + 128],
                                rel[po:po + 64, 512 * qc:512 * qc + 512],
                                start=False, stop=True), inc=True)
                        else:
                            for s_ in range(4):
                                qg = 4 * qc + s_
                                if s_ == sub_d:
                                    rel = Ub
                                elif s_ == sub_s:
                                    rel = V2b
                                elif qg > kc:
                                    rel = Ub
                                else:
                                    rel = V2b
                                sub = pbt[:, 128 * s_:128 * s_ + 128]
                                p.T(lambda e, kc=kc, qg=qg, s_=s_, sub=sub, po=po, rel=rel:
                                    e.matmul(
                                        sub,
                                        PHIR[po:po + 64, 128 * kc:128 * kc + 128],
                                        rel[po:po + 64, 128 * qg:128 * qg + 128],
                                        start=False, stop=(s_ == 3)),
                                    inc=(s_ == 3))
                            if 0 <= sub_d < 4:
                                d2c = 1024 + 128 * e2
                                p.T(lambda e, kc=kc, po=po, d2c=d2c, jc=jc: e.matmul(
                                    psN[:, d2c:d2c + 128],
                                    KTB[po:po + 64, jc, 128 * kc:128 * kc + 128],
                                    QUT[po:po + 64, jc, 128 * kc:128 * kc + 128],
                                    start=True, stop=False),
                                    waits=[("tS", last_sfix[0])] if e2 == 0 else ())
                                p.T(lambda e, kc=kc, po=po, V2b=V2b, d2c=d2c: e.matmul(
                                    psN[:, d2c:d2c + 128],
                                    PHIR[po:po + 64, 128 * kc:128 * kc + 128],
                                    V2b[po:po + 64, 128 * kc:128 * kc + 128],
                                    start=False, stop=True), inc=(e2 == 1))
                                if e2 == 1:
                                    d2_mm_t[kc] = p.cnt["tT"]
                            if 0 <= sub_d <= 4:
                                if sub_d == 0:
                                    q0b, nq, c0 = kc, 1, 256 * e2 + 128
                                elif sub_d == 4:
                                    q0b, nq, c0 = kc - 1, 1, 256 * e2
                                else:
                                    q0b, nq, c0 = kc - 1, 2, 256 * e2
                                p.T(lambda e, kc=kc, po=po, jc=jc, q0b=q0b, nq=nq, c0=c0:
                                    e.matmul(
                                        psR[2][:, c0:c0 + 128 * nq],
                                        KTB[po:po + 64, jc, 128 * kc:128 * kc + 128],
                                        QUT[po:po + 64, jc, 128 * q0b:128 * (q0b + nq)],
                                        start=True, stop=True),
                                    waits=[("tS", last_e3[0])] if e2 == 0 else (),
                                    inc=(e2 == 1))
                                if e2 == 1:
                                    d3_mm_t[kc] = p.cnt["tT"]
                        return sub_d, sub_s, p.cnt["tT"]

                    def emit_band_exps(kc, sub_d):
                        if not (0 <= sub_d <= 4):
                            return
                        if sub_d < 4:
                            p.Sc(lambda e: e.activation(
                                E2S[:, 0:2, :].rearrange("p a c -> p (a c)"),
                                psN[:, 1024:1280], AF.Exp, scale=0.125),
                                waits=[("tT", d2_mm_t[kc])], inc=True, drain=True)
                            last_sfix[0] = p.cnt["tS"]
                            sfix_t[(kc, 0)] = sfix_t[(kc, 1)] = p.cnt["tS"]
                        p.Sc(lambda e: e.activation(
                            E3S[:, 0:2, :].rearrange("p a c -> p (a c)"),
                            psR[2][:, 0:512], AF.Exp, scale=0.125),
                            waits=[("tT", d3_mm_t[kc])], inc=True, drain=True)
                        last_e3[0] = p.cnt["tS"]
                        e3_t[(kc, 0)] = e3_t[(kc, 1)] = p.cnt["tS"]

                    def emit_pair_exp(kc, tsc):
                        bidx = kc % 2
                        slot0 = (2 * kc) % 4
                        wexp = [("tT", tsc)]
                        if kc >= 2:
                            wexp.append(("tT", av_t[(kc - 2, 0)]))
                            wexp.append(("tT", av_t[(kc - 2, 1)]))
                        pw = (psRR, psXX)[bidx]
                        p.Sc(lambda e, pw=pw, slot0=slot0: e.activation(
                            ATT[:, slot0:slot0 + 2, :].rearrange("p s c -> p (s c)"),
                            pw[:], AF.Exp, scale=0.125),
                            waits=wexp, inc=True)
                        exp_t[(kc, 0)] = p.cnt["tS"]
                        exp_t[(kc, 1)] = p.cnt["tS"]
                        bank_last_exp[bidx] = p.cnt["tS"]

                    def emit_fixups(kc, e2, sub_d, sub_s):
                        slot = (2 * kc + e2) % 4
                        vfix = 0
                        if 0 <= sub_d < 4:
                            sfix = sfix_t[(kc, e2)]
                            p.V(lambda e, slot=slot, sub_d=sub_d, e2=e2: e.select(
                                ATT[:, slot, 128 * sub_d:128 * sub_d + 128], msel[:],
                                ATT[:, slot, 128 * sub_d:128 * sub_d + 128], E2S[:, e2, :]),
                                waits=[("tS", sfix), ("tS", exp_t[(kc, e2)]),
                                       ("tS", e3_t[(kc, e2)])])
                            p.V(lambda e, slot=slot, sub_d=sub_d, e2=e2: e.select(
                                ATT[:, slot, 128 * sub_d:128 * sub_d + 128], mlinv[:],
                                ATT[:, slot, 128 * sub_d:128 * sub_d + 128],
                                E3S[:, e2, 128:256]), inc=True, drain=True)
                            vfix = p.cnt["tV"]
                        if 0 <= sub_s < 4:
                            p.V(lambda e, slot=slot, sub_s=sub_s, e2=e2: e.select(
                                ATT[:, slot, 128 * sub_s:128 * sub_s + 128], msupv[:],
                                ATT[:, slot, 128 * sub_s:128 * sub_s + 128],
                                E3S[:, e2, 0:128]),
                                waits=[("tS", exp_t[(kc, e2)]), ("tS", e3_t[(kc, e2)])],
                                inc=True, drain=True)
                            vfix = p.cnt["tV"]
                        fix_v[(kc, e2)] = vfix

                    def emit_av(kc, e2):
                        hl = 2 * jc + e2
                        wav = [("tS", exp_t[(kc, e2)])]
                        if fix_v[(kc, e2)]:
                            wav.append(("tV", fix_v[(kc, e2)]))
                        if kc == 0:
                            wav.extend(av_free[e2])
                        slot = (2 * kc + e2) % 4
                        p.T(lambda e, kc=kc, hl=hl, slot=slot, e2=e2: e.matmul(
                            psN[0:65, 512 * e2:512 * e2 + 512],
                            VPB[:, kc, 65 * hl:65 * hl + 65],
                            ATT[:, slot, :], start=(kc == 0), stop=(kc == NT - 1)),
                            inc=True, waits=wav)
                        av_t[(kc, e2)] = p.cnt["tT"]

                    for kc in range(NT):
                        if kc == 2 and pending_tail[0] is not None:
                            pending_tail[0]()
                            pending_tail[0] = None
                        d0, s0_, _ = emit_scores(kc, 0)
                        d1, s1_, tsc = emit_scores(kc, 1)
                        if kc >= 2:
                            emit_av(kc - 2, 0)
                            emit_av(kc - 2, 1)
                        emit_band_exps(kc, d0)
                        emit_pair_exp(kc, tsc)
                        emit_fixups(kc, 0, d0, s0_)
                        emit_fixups(kc, 1, d1, s1_)
                        if half == 0 and jc == 0 and qc == 0 and kc == 0:
                            dump("ATT00", ATT[:], BF16)
                            dump("UT00", Ub[:], BF16)
                            dump("V200", V2b[:], BF16)
                            dump("QUT0", QUT[:], BF16)
                            dump("KTB0", KTB[:], BF16)
                            dump("PHIR0", PHIR[:], BF16)
                        if half == 0 and jc == 0 and qc == 0 and kc == 8:
                            dump("ATT08", ATT[:], BF16)
                    for kc in (NT - 2, NT - 1):
                        emit_av(kc, 0)
                        emit_av(kc, 1)
                    p.Sc(lambda e: e.activation(RSCR[64:65, :], psN[64:65, 0:512],
                                                AF.Ln),
                         waits=[("tT", av_t[(NT - 1, 0)])])
                    p.Sc(lambda e: e.activation(TMPE[64:65, :], RSCR[64:65, :],
                                                AF.Exp, scale=-1.0),
                         inc=True, drain=True)
                    srec0 = p.cnt["tS"]
                    p.Sc(lambda e: e.activation(RSCR[64:65, :], psN[64:65, 512:1024],
                                                AF.Ln),
                         waits=[("tT", av_t[(NT - 1, 1)])])
                    p.Sc(lambda e: e.activation(TMPF[64:65, :], RSCR[64:65, :],
                                                AF.Exp, scale=-1.0),
                         inc=True, drain=True)
                    srec1 = p.cnt["tS"]
                    p.V(lambda e: e.tensor_copy(TMPF[0:64, :], psN[0:64, 0:512]),
                        waits=[("tT", av_t[(NT - 1, 0)])], inc=True)
                    av_free[0] = [("tV", p.cnt["tV"]), ("tS", srec0)]
                    p.V(lambda e: e.tensor_copy(TMPE[0:64, :], psN[0:64, 512:1024]),
                        waits=[("tT", av_t[(NT - 1, 1)])], inc=True)
                    av_free[1] = [("tV", p.cnt["tV"]), ("tS", srec1)]
                    p.V(lambda e: e.tensor_copy(ODDS[64:65, 0, :], TMPE[64:65, :]),
                        waits=[("tS", srec0), ("tT", last_tb[0])], drain=True)
                    p.V(lambda e: e.tensor_sub(ODDS[64:65, 1, :], TMPE[64:65, :],
                                               ODDS[64:65, 0, :]), inc=True, drain=True)
                    p.V(lambda e: e.tensor_copy(XNS[64:65, 0:512], TMPF[64:65, :]),
                        waits=[("tS", srec1)], drain=True)
                    p.V(lambda e: e.tensor_sub(XNS[64:65, 512:1024], TMPF[64:65, :],
                                               XNS[64:65, 0:512]), inc=True, drain=True)
                    rec_done[0] = p.cnt["tV"]

                    def mk_tail(jcg=jcg, qc=qc):
                        def tail():
                            for e2 in range(2):
                                hi_ap = ODDS[64:65, 0, :] if e2 == 0 else XNS[64:65, 0:512]
                                lo_ap = ODDS[64:65, 1, :] if e2 == 0 else XNS[64:65, 512:1024]
                                avs = TMPF if e2 == 0 else TMPE
                                bc0 = 512 * e2
                                tb = p.T(lambda e, hi_ap=hi_ap, bc0=bc0: e.matmul(
                                    psN[0:64, bc0:bc0 + 512], allob[64:65, 0:64],
                                    hi_ap, start=True, stop=False),
                                    waits=[("tV", rec_done[0]), ("tV", evac_done[e2])])
                                tb = p.T(lambda e, lo_ap=lo_ap, bc0=bc0: e.matmul(
                                    psN[0:64, bc0:bc0 + 512], allob[64:65, 0:64],
                                    lo_ap, start=False, stop=True), inc=True)
                                last_tb[0] = tb
                                if e2 == 0:
                                    p.V(lambda e, jcg=jcg, qc=qc, avs=avs, bc0=bc0:
                                        e.tensor_tensor(
                                            AOT[0:64, jcg, 512 * qc:512 * qc + 512],
                                            avs[0:64, :], psN[0:64, bc0:bc0 + 512],
                                            op=ALU.mult),
                                        waits=[("tT", tb)], inc=True)
                                    evac_done[0] = p.cnt["tV"]
                                    av_free[0] = [("tV", p.cnt["tV"])]
                                else:
                                    oslot = qc % 2
                                    p.V(lambda e, oslot=oslot, avs=avs, bc0=bc0:
                                        e.tensor_tensor(
                                            ODDS[0:64, oslot, :], avs[0:64, :],
                                            psN[0:64, bc0:bc0 + 512], op=ALU.mult),
                                        waits=[("tT", tb), ("dY", odd_slot_free[oslot])],
                                        inc=True)
                                    evac_done[1] = p.cnt["tV"]
                                    av_free[1] = [("tV", p.cnt["tV"])]
                                    ev = p.cnt["tV"]
                                    p.Y(lambda e, oslot=oslot, jcg=jcg, qc=qc: e.dma_start(
                                        AOT[64:128, jcg, 512 * qc:512 * qc + 512],
                                        ODDS[0:64, oslot, :]),
                                        waits=[("tV", ev)])
                                    odd_slot_free[oslot] = p.Yf()
                        return tail

                    pending_tail[0] = mk_tail()
            if pending_tail[0] is not None:
                pending_tail[0]()
                pending_tail[0] = None
            p.barrier()
        dump("AOTB", AOT[:], BF16)
        p.G(lambda e: e.dma_start(XNS[0:1, :], d["bo"]))
        tbo = p.Gf()
        tb1 = p.T(lambda e: e.matmul(psR[2][:, 0:512], allob[0:1, :], XNS[0:1, 0:512],
                                     start=True, stop=True), waits=[("dG", tbo)])
        tb1 = p.T(lambda e: e.matmul(psN[:, 0:512], allob[0:1, :], XNS[0:1, 512:1024],
                                     start=True, stop=True), inc=True)
        p.V(lambda e: e.tensor_copy(BOBC[:, 0:512], psR[2][:, 0:512]),
            waits=[("tT", tb1)], inc=True)
        p.V(lambda e: e.tensor_copy(BOBC[:, 512:1024], psN[:, 0:512]), inc=True)
        emit_wo(wvw["wo_b"], xs_v, add_bo=True, st1=wo_st1[0])

    def emit_store():
        for t in range(NT):
            p.Y(lambda e, t=t: e.dma_start(out_v[:, t, :], xtile(t)))
        p.Yf()

    dump("X0", XT0[:], F32)
    dump("X1", XT1[:], F32)
    emit_ln(skip_stats=True)
    emit_attn_a()
    if cut == "attn_a":
        emit_store()
    else:
        emit_ffn(spill_x=True)
        if cut == "ffn1":
            emit_store()
        else:
            emit_attn_b()
            if cut == "attn_b":
                emit_store()
            else:
                emit_ffn()
                emit_store()

    from contextlib import ExitStack
    with ExitStack() as cm:
        E = cm.enter_context
        p.sems = {k: E(nc.semaphore(k)) for k in ("tT", "tV", "tS", "dY", "dG", "dS")}
        blk = E(nc.Block())
        blk.tensor(lambda eng: [f(eng) for f in p.ops["T"]])
        blk.vector(lambda eng: [f(eng) for f in p.ops["V"]])
        blk.scalar(lambda eng: [f(eng) for f in p.ops["Sc"]])
        blk.sync(lambda eng: [f(eng) for f in p.ops["Y"]])
        blk.gpsimd(lambda eng: [f(eng) for f in p.ops["G"]])
    return nc


def prep_inputs(inp):
    c = make_consts()
    common = {
        "wq_a": inp["L0_attn_a_Wq"], "wk_a": inp["L0_attn_a_Wk"],
        "wv_a": inp["L0_attn_a_Wv"], "wo_a": inp["L0_attn_a_Wo"],
        "wa_a": inp["L0_attn_a_Wa"],
        "ba": np.asarray(inp["L0_attn_a_ba"], np.float32).reshape(1, H),
        "wr_t": np.ascontiguousarray(np.asarray(inp["L0_ffn_W_r"], np.float32).T),
        "wk_t": np.ascontiguousarray(np.asarray(inp["L0_ffn_W_k"], np.float32).T),
        "wv_t": np.ascontiguousarray(np.asarray(inp["L0_ffn_W_v"], np.float32).T),
        "mu_r8": _chunk8(np.asarray(inp["L0_ffn_mu_r"], np.float32)),
        "mu_k8": _chunk8(np.asarray(inp["L0_ffn_mu_k"], np.float32)),
        "wq_b": inp["L0_attn_b_Wq"], "wk_b": inp["L0_attn_b_Wk"],
        "wv_b": inp["L0_attn_b_Wv"], "wo_b": inp["L0_attn_b_Wo"],
        "u8": _chunk8(np.asarray(inp["L0_attn_b_u"], np.float32).reshape(-1)),
        "v8": _chunk8(np.asarray(inp["L0_attn_b_v"], np.float32).reshape(-1)),
        "bo": np.asarray(inp["L0_attn_b_bo"], np.float32).reshape(1, D),
    }
    common.update(c)
    common = {k: np.ascontiguousarray(v) for k, v in common.items()}
    xfull = np.asarray(inp["L0_attn_a_x"], np.float32)
    in_maps = [dict(common, x=np.ascontiguousarray(xfull[b])) for b in range(8)]
    return in_maps


def kernel(**inputs):
    in_maps = prep_inputs(inputs)
    nc = build_nc(CUT)
    res = run_bass_kernel_spmd(nc, in_maps, core_ids=list(range(8)))
    out = np.stack([np.asarray(r["out"]) for r in res.results], 0)
    return out.astype(np.float32)


# version 20
# speedup vs baseline: 1.2068x; 1.2068x over previous
import numpy as np
import ml_dtypes
import concourse.bass as bass
import concourse.mybir as mybir
from concourse.bass_utils import run_bass_kernel_spmd

F32 = mybir.dt.float32
BF16 = mybir.dt.bfloat16
U8 = mybir.dt.uint8
AF = mybir.ActivationFunctionType
ALU = mybir.AluOpType
BF = ml_dtypes.bfloat16

S = 2048
D = 1024
H = 16
DH = 64
NT = 16
NJ = 8
KB = 1024

CUT = None
PARANOID = False


def _hilo(x64):
    hi = x64.astype(np.float32)
    lo = (x64 - hi.astype(np.float64)).astype(np.float32)
    return hi, lo


def make_consts():
    c = {}
    i0 = np.arange(128)
    c["ident"] = np.eye(128, dtype=BF)
    c["triu"] = (i0[:, None] <= i0[None, :]).astype(BF)
    c["tincl"] = (i0[:, None] <= i0[None, :]).astype(np.float32)
    c["msel"] = (i0[:, None] <= i0[None, :]).astype(np.uint8)
    c["mline_d"] = (1.0 - (i0[:, None] == i0[None, :] + 1)).astype(BF)
    ml_s = np.ones((128, 128), dtype=np.float64)
    ml_s[0, 127] = 0.0
    c["mline_s"] = ml_s.astype(BF)
    c["mlineinv"] = (i0[:, None] != i0[None, :] + 1).astype(np.uint8)
    msupl = np.ones((128, 128), dtype=np.uint8)
    msupl[0, 127] = 0
    c["msuplinv"] = msupl
    m = np.arange(32, dtype=np.float64)
    w = 1.0 / 10000.0 ** (2.0 * m / 64.0)
    qpos = np.arange(S, dtype=np.float64)
    ang = w[:, None] * qpos[None, :]
    sinq = np.sin(ang); cosq = np.cos(ang)
    c["sinq_hi"], c["sinq_lo"] = _hilo(np.concatenate([sinq] * 4, 0))
    c["cosq_hi"], c["cosq_lo"] = _hilo(np.concatenate([cosq] * 4, 0))
    phi = np.concatenate([cosq, sinq], 0)
    c["phi_hi"], c["phi_lo"] = _hilo(np.concatenate([phi] * 2, 0))
    cS = np.cos(S * w); sS = np.sin(S * w)
    c["cs_hi"], c["cs_lo"] = _hilo(np.concatenate([cS] * 4)[:, None])
    c["ss_hi"], c["ss_lo"] = _hilo(np.concatenate([-sS, sS, -sS, sS])[:, None])
    c["sgn"] = np.concatenate([np.ones(32), -np.ones(32), np.ones(32), -np.ones(32)]).astype(np.float32)[:, None]
    return c


def _chunk8(vec):
    return np.ascontiguousarray(vec.reshape(8, 128).T)


def bc_last(ap, n):
    return bass.AP(ap.tensor, ap.offset, list(ap.ap) + [[0, n]])


def sq_last(ap):
    a = list(ap.ap)
    assert a[-1][1] == 1, a
    return bass.AP(ap.tensor, ap.offset, a[:-1])


ENGS = ("T", "V", "Sc", "Y", "G")
EKEY = {"T": "tT", "V": "tV", "Sc": "tS", "Y": "dY", "G": "dG"}


class Plan:
    def __init__(self):
        self.ops = {e: [] for e in ENGS}
        self.cnt = {"tT": 0, "tV": 0, "tS": 0, "dY": 0, "dG": 0, "dS": 0}
        self.waited = {e: dict() for e in ENGS}
        self.dirty = {e: False for e in ("T", "V", "Sc")}
        self.sems = None

    def _add(self, eng, semkey, fn, waits, inc, drain=False):
        wl = []
        for sname, val in waits:
            if val > 0 and val > self.waited[eng].get(sname, 0):
                self.waited[eng][sname] = val
                wl.append((sname, val))
        tick = None
        if inc:
            self.cnt[semkey] += 16 if semkey.startswith("d") else 1
            tick = self.cnt[semkey]
            if eng in self.dirty:
                self.dirty[eng] = False
        elif eng in self.dirty:
            self.dirty[eng] = True
        plan = self

        def closure(e, wl=tuple(wl), fn=fn, tick=tick, semkey=semkey, drain=drain):
            for sname, val in wl:
                e.wait_ge(plan.sems[sname], val)
            ins = fn(e)
            if drain:
                ins = e.drain()
            if tick is not None:
                ins.then_inc(plan.sems[semkey], 16 if semkey.startswith("d") else 1)
        self.ops[eng].append(closure)
        return self.cnt[semkey]

    def T(self, fn, waits=(), inc=False):
        return self._add("T", "tT", fn, waits, inc)

    def V(self, fn, waits=(), inc=False, drain=False):
        return self._add("V", "tV", fn, waits, inc, drain)

    def Sc(self, fn, waits=(), inc=False, drain=False):
        return self._add("Sc", "tS", fn, waits, inc, drain)

    def Y(self, fn, waits=()):
        return self._add("Y", "dY", fn, waits, True)

    def G(self, fn, waits=()):
        return self._add("G", "dG", fn, waits, True)

    def Yf(self):
        v = self.cnt["dY"]
        plan = self
        if v > self.waited["Y"].get("dY", 0):
            self.waited["Y"]["dY"] = v
            self.ops["Y"].append(lambda e, v=v: e.wait_ge(plan.sems["dY"], v))
        return v

    def Gf(self):
        v = self.cnt["dG"]
        plan = self
        if v > self.waited["G"].get("dG", 0):
            self.waited["G"]["dG"] = v
            self.ops["G"].append(lambda e, v=v: e.wait_ge(plan.sems["dG"], v))
        return v

    def fence(self, eng):
        if self.dirty.get(eng):
            key = EKEY[eng]
            self.cnt[key] += 1
            tick = self.cnt[key]
            plan = self

            def closure(e, tick=tick, key=key):
                e.nop().then_inc(plan.sems[key], 1)
            self.ops[eng].append(closure)
            self.dirty[eng] = False

    def barrier(self):
        for e in ("T", "V", "Sc"):
            self.fence(e)
        self.Yf()
        self.Gf()
        snap = [(k, v) for k, v in self.cnt.items() if v > 0]
        plan = self
        for e in ENGS:
            for k, v in snap:
                if v > self.waited[e].get(k, 0):
                    self.waited[e][k] = v

            def closure(eng, snap=tuple(snap)):
                for k, v in snap:
                    eng.wait_ge(plan.sems[k], v)
            self.ops[e].append(closure)


def cn(h):
    return 512 * (h // 7) + 65 * (h % 7)


def build_nc(cut=None, dbg=False):
    nc = bass.Bass("TRN2", target_bir_lowering=False, debug=False)

    d = {}
    def din(name, shape, dtype=F32):
        d[name] = nc.dram_tensor(name, list(shape), dtype, kind="ExternalInput").ap()
        return d[name]

    x_d = din("x", (S, D))
    for nm in ("wq_a", "wk_a", "wv_a", "wo_a", "wr_t", "wk_t", "wv_t",
               "wq_b", "wk_b", "wv_b", "wo_b"):
        din(nm, (D, D))
    din("wa_a", (D, H))
    din("ba", (1, H)); din("bo", (1, D))
    din("mu_r8", (128, 8)); din("mu_k8", (128, 8))
    din("u8", (128, 8)); din("v8", (128, 8))
    din("ident", (128, 128), BF16); din("triu", (128, 128), BF16)
    din("tincl", (128, 128), F32); din("msel", (128, 128), U8)
    din("mline_d", (128, 128), BF16); din("mline_s", (128, 128), BF16)
    din("mlineinv", (128, 128), U8); din("msuplinv", (128, 128), U8)
    for nm in ("sinq_hi", "sinq_lo", "cosq_hi", "cosq_lo", "phi_hi", "phi_lo"):
        din(nm, (128, S))
    for nm in ("cs_hi", "cs_lo", "ss_hi", "ss_lo", "sgn"):
        din(nm, (128, 1))
    xspill_d = nc.dram_tensor("xspill", [S, D], F32, kind="Internal").ap()
    out_d = nc.dram_tensor("out", [S, D], F32, kind="ExternalOutput").ap()
    dbg_outs = {}
    def dump(name, sb_ap, dtype):
        if not dbg or name in dbg_outs:
            return
        shape = list(sb_ap.shape)
        dd = nc.dram_tensor("dbg_" + name, shape, dtype, kind="ExternalOutput").ap()
        dbg_outs[name] = dd
        p.barrier()
        tk = p.Y(lambda e, dd=dd, sb_ap=sb_ap: e.dma_start(dd, sb_ap))
        p.barrier()

    def tv(ap):
        return ap.rearrange("(t p) c -> p t c", p=128)
    x_v, xs_v, out_v = tv(x_d), tv(xspill_d), tv(out_d)
    wvw = {nm: d[nm].rearrange("(j p) c -> p j c", p=128)
           for nm in ("wq_a", "wk_a", "wv_a", "wo_a", "wr_t", "wk_t", "wv_t",
                      "wq_b", "wk_b", "wv_b", "wo_b")}
    wa_v = d["wa_a"].rearrange("(j p) c -> p j c", p=128)

    alloc = nc.alloc_sbuf_tensor

    def place(arena, off, name, shape, dtype):
        return nc.alloc_sbuf_tensor_at(
            name, list(shape), dtype, offset=nc.lookup_mloc(arena).addr + off)

    SL1 = alloc("SL1", [128, 32768], U8)
    SL2 = alloc("SL2", [128, 32768], U8)
    SL3 = alloc("SL3", [128, 32768], U8)
    SL4 = alloc("SL4", [128, 33280], U8)
    XAR = alloc("XAR", [128, 32768], U8)
    SCR = alloc("SCR", [128, 10240], U8)
    GB = alloc("GB", [128, 4096], U8)

    XNT = place(SL1, 0, "XNT", [128, NJ, S], BF16)
    KAT = place(SL1, 0, "KAT", [128, NJ, S], BF16)
    XOLD = place(SL1, 0, "XOLD", [128, 4, KB], F32)
    XT0 = place(SL2, 0, "XT0", [128, 8, KB], F32)
    QA = place(SL2, 0, "QA", [128, NT, KB], BF16)
    QUT = place(SL2, 0, "QUT", [128, 4, S], BF16)
    KTB = place(SL2, 16384, "KTB", [128, 4, S], BF16)
    XT1 = place(SL3, 0, "XT1", [128, 8, KB], F32)
    KA = place(SL3, 0, "KA", [128, NT, KB], BF16)
    VPB = place(SL3, 0, "VPB", [128, NT, 8 * 65], BF16)
    UTA = place(SL3, 16640, "UTA", [128, S], BF16)
    V2A = place(SL3, 20736, "V2A", [128, S], BF16)
    VPA = place(SL4, 0, "VPA", [128, NT, H * 65], BF16)
    RG = place(SL4, 0, "RG", [128, NT, KB], BF16)
    SINR = place(SL4, 0, "SINR", [128, S], BF16)
    COSR = place(SL4, 4096, "COSR", [128, S], BF16)
    PHIR = place(SL4, 8192, "PHIR", [128, S], BF16)
    QVT = place(SL4, 12288, "QVT", [128, S], BF16)
    SWP = place(SL4, 16384, "SWP", [128, S], BF16)
    UTB = place(SL4, 20480, "UTB", [128, S], BF16)
    V2B = place(SL4, 24576, "V2B", [128, S], BF16)
    AOT = place(XAR, 0, "AOT", [128, NJ, S], BF16)
    XRB = place(XAR, 0, "XRB", [128, NJ, S], BF16)
    TBSTG = place(XAR, 0, "TBSTG", [128, 2, S], F32)
    TBF2 = TBSTG[:, 1, 0:512]
    TBE2 = TBSTG[:, 1, 512:1024]
    QTb = place(SCR, 0, "QTb", [64, H, 128], BF16)
    KTb = place(SCR, 4096, "KTb", [64, H, 128], BF16)
    AOC = place(SCR, 8192, "AOC", [128, KB], BF16)
    GT0 = place(SCR, 8192, "GT0", [128, H], F32)
    GT1 = place(SCR, 8320, "GT1", [128, H], F32)
    TMPD = place(SCR, 0, "TMPD", [128, S], BF16)
    ATT = place(SCR, 0, "ATT", [128, 4, 512], BF16)
    E2S = place(SCR, 4096, "E2S", [128, 2, 128], BF16)
    E3S = place(SCR, 6656, "E3S", [128, 2, 256], BF16)
    ODDS = place(SCR, 4608, "ODDS", [128, 2, 512], BF16)
    RSCR = place(SCR, 7680, "RSCR", [128, 512], F32)
    SQ = place(GB, 0, "SQ", [128, NT, H], F32)
    SK = place(GB, 1024, "SK", [128, NT, H], F32)
    LAM = place(GB, 2048, "LAM", [128, NT, H], F32)
    AALL = place(GB, 3072, "AALL", [128, NT, H], F32)
    BOBC = place(GB, 0, "BOBC", [128, KB], F32)

    ident = alloc("identb", [128, 128], BF16)
    allob = alloc("allob", [128, 128], BF16)
    triub = alloc("triub", [128, 128], BF16)
    tincl = alloc("tinclb", [128, 128], F32)
    allof = alloc("allof", [128, 128], F32)
    msel = alloc("mselb", [128, 128], U8)
    mlind = alloc("mlind", [128, 128], BF16)
    mlins = alloc("mlins", [128, 128], BF16)
    mlinv = alloc("mlinv", [128, 128], U8)
    msupv = alloc("msupv", [128, 128], U8)
    mur = alloc("mur", [128, 8], F32)
    muk = alloc("muk", [128, 8], F32)
    u8b = alloc("u8b", [128, 8], F32)
    v8b = alloc("v8b", [128, 8], F32)
    vmu = alloc("vmu", [128, 8], F32)
    csv = alloc("csv", [128, 1], F32)
    ssv = alloc("ssv", [128, 1], F32)
    sgn = alloc("sgnb", [128, 1], F32)
    hl0 = alloc("hl0", [128, 1], F32)
    bab = alloc("bab", [128, H], F32)
    wasb = alloc("wasb", [128, NJ, H], BF16)
    WS = [alloc("WS0", [128, NJ, 512], BF16), alloc("WS1", [128, NJ, 512], BF16)]
    SF = alloc("SF", [64, H * 65], F32)
    SBb = alloc("SBb", [64, H * 65], BF16)
    XNS = alloc("XNS", [128, KB], BF16)
    MTS = alloc("MTS", [128, 2, 128], BF16)
    TMPF = alloc("TMPF", [128, 512], F32)
    TMPE = alloc("TMPE", [128, 512], F32)
    MV = alloc("MV", [128, NT, 2], F32)
    MV2 = alloc("MV2", [128, NT, 4], F32)
    ST12 = alloc("ST12", [128, 12], F32)
    RS = alloc("RSb", [128, NT], F32)
    NMS = alloc("NMSb", [128, NT], F32)
    DENS = alloc("DENS", [128, 16], F32)
    RECS = alloc("RECS", [128, 16], F32)
    CEPS = alloc("CEPS", [128, 1], F32)
    nc.const_aps.register(F32, 1e-5, CEPS[:, 0:1])

    psR = [nc.alloc_psum_tensor(f"psR{i}", [128, 512], F32) for i in range(3)]
    psX = [nc.alloc_psum_tensor(f"psX{i}", [128, 1024], BF16) for i in range(2)]
    psN = nc.alloc_psum_tensor("psN", [128, 1536], F32)
    _pb = nc.psum_base
    nc.psum_base = 0
    psRR = nc.alloc_psum_tensor("psRR", [128, 1024], F32)
    nc.psum_base = 3
    psXX = nc.alloc_psum_tensor("psXX", [128, 1024], F32)
    nc.psum_base = 3
    psXA = nc.alloc_psum_tensor("psXA", [128, 512], F32)
    psXB = nc.alloc_psum_tensor("psXB", [128, 512], F32)
    nc.psum_base = _pb

    p = Plan()

    def xtile(t):
        return (XT0 if t < 8 else XT1)[:, t % 8, :]

    def emit_ln_stats(t, waits=()):
        p.V(lambda e, t=t: e.bn_stats(ST12[:, 0:6], xtile(t)[:, 0:512]),
            waits=waits, drain=True)
        p.V(lambda e, t=t: e.bn_stats(ST12[:, 6:12], xtile(t)[:, 512:1024]), drain=True)
        p.V(lambda e, t=t: e.bn_aggr(MV2[:, t, 0:2], ST12[:, 0:6]), drain=True)
        p.V(lambda e, t=t: e.bn_aggr(MV2[:, t, 2:4], ST12[:, 6:12]), inc=True, drain=True)

    xload = []
    xf = {}
    for t in range(NT):
        xload.append(p.Y(lambda e, t=t: e.dma_start(xtile(t), x_v[:, t, :])))
        if t % 4 == 3:
            xf[t // 4] = p.Yf()
    for dst, src in ((ident, "ident"), (triub, "triu"), (tincl, "tincl"),
                     (msel, "msel"), (mlind, "mline_d"), (mlins, "mline_s"),
                     (mlinv, "mlineinv"), (msupv, "msuplinv"),
                     (mur, "mu_r8"), (muk, "mu_k8"), (u8b, "u8"), (v8b, "v8"),
                     (sgn, "sgn")):
        p.Y(lambda e, dst=dst, src=src: e.dma_start(dst[:], d[src]))
    p.Y(lambda e: e.dma_start(csv[:], d["cs_hi"]))
    p.Y(lambda e: e.dma_start(hl0[:], d["cs_lo"]))
    t1 = p.Yf()
    v1 = p.V(lambda e: e.tensor_add(csv[:], csv[:], hl0[:]), waits=[("dY", t1)], inc=True, drain=True)
    p.Y(lambda e: e.dma_start(ssv[:], d["ss_hi"]), waits=[("tV", v1)])
    p.Y(lambda e: e.dma_start(hl0[:], d["ss_lo"]))
    t1 = p.Yf()
    p.V(lambda e: e.tensor_add(ssv[:], ssv[:], hl0[:]), waits=[("dY", t1)], inc=True, drain=True)
    p.V(lambda e: e.tensor_sub(vmu[:], v8b[:], u8b[:]),
        waits=[("dY", t1)], inc=True, drain=True)
    for j in range(NJ):
        p.G(lambda e, j=j: e.dma_start(wasb[:, j, :], wa_v[:, j, :]))
    p.V(lambda e: e.memset(allob[:], 1.0), inc=True)
    p.V(lambda e: e.memset(allof[:], 1.0), inc=True)
    p.V(lambda e: e.memset(CEPS[:], 1e-5), inc=True, drain=True)
    p.G(lambda e: e.dma_start(XNS[0:1, 0:H], d["ba"]))
    tba = p.Gf()
    tmm = p.T(lambda e: e.matmul(psR[2][:, 0:H], allob[0:1, :], XNS[0:1, 0:H],
                                 start=True, stop=True),
              waits=[("dG", tba), ("tV", p.cnt["tV"])], inc=True)
    p.V(lambda e: e.tensor_scalar(bab[:], psR[2][:, 0:H], 2.0, None, op0=ALU.add),
        waits=[("tT", tmm)], inc=True, drain=True)
    for t in range(NT):
        emit_ln_stats(t, waits=[("dY", xf[t // 4])] if t % 4 == 0 else ())
    p.barrier()

    def wslab(slot, wview, half, waits=()):
        for j in range(NJ):
            p.G(lambda e, j=j, s=slot, w=wview, h=half:
                e.dma_start(WS[s][:, j, :], w[:, j, 512 * h:512 * h + 512]),
                waits=waits if j == 0 else ())
        return p.Gf()

    def emit_ln(end_barrier=True, skip_stats=False):
        if not skip_stats:
            for t in range(NT):
                emit_ln_stats(t)
        m1 = sq_last(MV2[:, :, 0:1])
        v1 = sq_last(MV2[:, :, 1:2])
        m2 = sq_last(MV2[:, :, 2:3])
        v2 = sq_last(MV2[:, :, 3:4])
        mean_v = sq_last(MV[:, :, 0:1])
        var_v = sq_last(MV[:, :, 1:2])
        A_, B_, C_, D_ = RS[:], NMS[:], DENS[:], RECS[:]
        p.V(lambda e: e.tensor_add(A_, m1, m2), drain=True)
        p.V(lambda e: e.tensor_sub(B_, m1, m2), drain=True)
        p.V(lambda e: e.tensor_add(D_, v1, v2), drain=True)
        p.V(lambda e: e.tensor_scalar_mul(mean_v, A_, 0.5), drain=True)
        p.V(lambda e: e.tensor_mul(C_, B_, B_), drain=True)
        p.V(lambda e: e.tensor_scalar_mul(A_, D_, 0.5), drain=True)
        p.V(lambda e: e.tensor_scalar_mul(B_, C_, 0.25), drain=True)
        p.V(lambda e: e.tensor_add(var_v, A_, B_), inc=True, drain=True)
        vstats = p.cnt["tV"]
        p.Sc(lambda e: e.activation(NMS[:], var_v, AF.Ln, bias=1e-5),
             waits=[("tV", vstats)], drain=True)
        srs = p.Sc(lambda e: e.activation(RS[:], NMS[:], AF.Exp, scale=-0.5),
                   inc=True, drain=True)
        copy_t = {}
        trans_t = {}
        for t in range(NT):
            w = [("tS", srs)]
            if t > 0:
                w.append(("tT", trans_t[t - 1]))
            p.V(lambda e, t=t: e.tensor_scalar(
                XNS[:], xtile(t), MV[:, t, 0:1], RS[:, t:t + 1],
                op0=ALU.subtract, op1=ALU.mult), waits=w, inc=True)
            vxn = p.cnt["tV"]
            bank = psX[t % 2]
            wb = [("tV", vxn)]
            if t >= 2:
                wb.append(copy_t[t - 2])
            for j in range(NJ):
                p.T(lambda e, j=j, bank=bank: e.transpose(
                    bank[:, 128 * j:128 * j + 128], XNS[:, 128 * j:128 * j + 128],
                    ident[:]), waits=wb if j == 0 else (), inc=(j == NJ - 1))
            trans_t[t] = p.cnt["tT"]
            p.Sc(lambda e, t=t, bank=bank: e.copy(
                XNT[:, :, 128 * t:128 * t + 128],
                bank[:].rearrange("p (j c) -> p j c", c=128)),
                waits=[("tT", trans_t[t])], inc=True, drain=True)
            copy_t[t] = ("tS", p.cnt["tS"])
        dump("XNT", XNT[:], BF16)
        dump("MV", MV[:], F32)
        dump("MV2", MV2[:], F32)
        dump("RS", RS[:], F32)
        if end_barrier:
            p.barrier()
        return copy_t

    def emit_attn_a():
        pre_slab = {0: wslab(0, wvw["wq_a"], 0), 1: wslab(1, wvw["wq_a"], 1)}
        vprev = {0: 0, 1: 0}
        sprev = {0: 0, 1: 0}
        for t in range(NT):
            gpb = psR[2][:, 0:H] if t % 2 == 0 else psN[:, 0:H]
            gt = GT0 if t % 2 == 0 else GT1
            for j in range(NJ):
                p.T(lambda e, t=t, j=j, gpb=gpb: e.matmul(
                    gpb, XNT[:, j, 128 * t:128 * t + 128], wasb[:, j, :],
                    start=(j == 0), stop=(j == NJ - 1)),
                    waits=([("tV", vprev[t % 2])] if j == 0 else ()), inc=(j == NJ - 1))
            tmm = p.cnt["tT"]
            vprev[t % 2] = p.V(lambda e, gt=gt, gpb=gpb: e.tensor_add(
                gt[:], gpb, bab[:]),
                waits=[("tT", tmm), ("tS", sprev[t % 2])], inc=True, drain=True)
            sprev[t % 2] = p.Sc(lambda e, t=t, gt=gt: e.activation(
                AALL[:, t, :], gt[:], AF.Sigmoid),
                waits=[("tV", vprev[t % 2])], inc=True, drain=True)
        aall_flat = AALL[:, :, :].rearrange("p t h -> p (t h)")
        sln = p.Sc(lambda e: e.activation(aall_flat, aall_flat, AF.Ln), inc=True, drain=True)
        sprev = sln
        for t in range(NT):
            tcs = p.T(lambda e, t=t: e.matmul(psR[2][:, 16:32], tincl[:], AALL[:, t, :],
                                              start=True, stop=True),
                      waits=[("tS", sprev)])
            tcs = p.T(lambda e, t=t: e.matmul(psR[2][:, 32:48], allof[:], AALL[:, t, :],
                                              start=False, stop=True), inc=True)
            p.Sc(lambda e, t=t: e.activation(SQ[:, t, :], psR[2][:, 16:32], AF.Exp),
                 waits=[("tT", tcs)], drain=True)
            p.Sc(lambda e, t=t: e.activation(SK[:, t, :], psR[2][:, 16:32], AF.Exp, scale=-1.0),
                 drain=True)
            sprev = p.Sc(lambda e, t=t: e.activation(LAM[:, t, :], psR[2][:, 32:48], AF.Exp),
                         inc=True, drain=True)
        dump("SQ", SQ[:], F32); dump("SK", SK[:], F32)
        dump("LAM", LAM[:], F32); dump("AALL", AALL[:], F32)
        p.barrier()

        p.V(lambda e: e.memset(VPA[:], 1.0), inc=True)

        slab_seq = [("wq_a", 0), ("wq_a", 1), ("wk_a", 0), ("wk_a", 1),
                    ("wv_a", 0), ("wv_a", 1)]
        slab_tick = dict(pre_slab)
        evac_tick = {}
        elu_prev = [0]
        for i, (wnm, half) in enumerate(slab_seq):
            slot = i % 2
            for t in range(NT):
                pb = psR[t % 2]
                w = [("dG", slab_tick[i])]
                if (i, t - 2) in evac_tick:
                    w.append(evac_tick[(i, t - 2)])
                elif t == 0 and (i - 1, NT - 2) in evac_tick:
                    w.append(evac_tick[(i - 1, NT - 2)])
                elif t == 1 and (i - 1, NT - 1) in evac_tick:
                    w.append(evac_tick[(i - 1, NT - 1)])
                for j in range(NJ):
                    p.T(lambda e, t=t, j=j, pb=pb, slot=slot: e.matmul(
                        pb[:], XNT[:, j, 128 * t:128 * t + 128], WS[slot][:, j, :],
                        start=(j == 0), stop=(j == NJ - 1)),
                        waits=w if j == 0 else (), inc=(j == NJ - 1))
                tmm = p.cnt["tT"]
                if wnm in ("wq_a", "wk_a"):
                    bF = TMPF if t % 2 == 0 else TBF2
                    bE = TMPE if t % 2 == 0 else TBE2
                    p.Sc(lambda e, pb=pb, bF=bF: e.activation(
                        bF[:], pb[:], AF.Relu, scale=-1.0),
                        waits=w + [("tT", tmm)], inc=True)
                    p.Sc(lambda e, bF=bF, bE=bE: e.activation(
                        bE[:], bF[:], AF.Exp, scale=-1.0), inc=True)
                    sexp = p.cnt["tS"]
                    p.V(lambda e, pb=pb, bF=bF, bE=bE: e.scalar_tensor_tensor(
                        bF[:], pb[:], 0.0, bE[:], op0=ALU.max, op1=ALU.add),
                        waits=[("tS", sexp)], inc=True)
                    dstb = QA if wnm == "wq_a" else KA
                    scale = SQ if wnm == "wq_a" else SK
                    dv = dstb[:, t, 512 * half:512 * half + 512].rearrange(
                        "p (h c) -> p h c", c=64)
                    p.V(lambda e, dv=dv, t=t, scale=scale, half=half, bF=bF:
                        e.tensor_tensor(
                            dv, bF[:].rearrange("p (h c) -> p h c", c=64),
                            bc_last(scale[:, t, 8 * half:8 * half + 8], 64),
                            op=ALU.mult), inc=True, drain=True)
                    evac_tick[(i, t)] = ("tV", p.cnt["tV"])
                else:
                    vpa_dst = VPA[:, t, 65 * 8 * half:65 * 8 * half + 65 * 8]
                    vpa_dst = vpa_dst.rearrange("p (h c) -> p h c", c=65)[:, :, 0:64]
                    src = pb[:].rearrange("p (h c) -> p h c", c=64)
                    p.Sc(lambda e, vpa_dst=vpa_dst, src=src: e.copy(vpa_dst, src),
                         waits=[("tT", tmm)], inc=True, drain=True)
                    evac_tick[(i, t)] = ("tS", p.cnt["tS"])
            if i + 2 < len(slab_seq):
                wnm2, half2 = slab_seq[i + 2]
                slab_tick[i + 2] = wslab((i + 2) % 2, wvw[wnm2], half2,
                                         waits=[("tT", p.cnt["tT"])])
        dump("QA", QA[:], BF16); dump("KA", KA[:], BF16); dump("VPA", VPA[:], BF16)
        p.barrier()
        wslab(0, wvw["wo_a"], 0)
        wo_st1 = wslab(1, wvw["wo_a"], 1)

        p.V(lambda e: e.memset(SF[:], 0.0), inc=True, drain=True)
        p.V(lambda e: e.memset(SBb[:], 0.0), inc=True, drain=True)
        vS = ("tV", p.cnt["tV"])
        vO = {}
        vSu = ("tV", p.cnt["tV"])
        last_ao_copy = ("tV", 0)
        last_aot_tick = 0
        groups = [(0, 0, 7), (1, 7, 7), (2, 14, 2)]

        def emit_tail(c, gc_wait=None):
            nonlocal vS, vSu, last_ao_copy, last_aot_tick
            for h in range(H):
                bank_first = h in (0, 7, 14)
                p.T(lambda e, h=h, c=c, bf=bank_first: e.matmul(
                    psN[0:64, cn(h):cn(h) + 65], KA[:, c, 64 * h:64 * h + 64],
                    VPA[:, c, 65 * h:65 * h + 65], start=bf, stop=True),
                    waits=([("tV", vO[c])] if h == 0 else ()), inc=(h == H - 1))
            tS = p.cnt["tT"]
            for g, off, cnt_ in groups:
                view = psN[0:64, 512 * g:512 * g + 65 * cnt_].rearrange("p (h c) -> p h c", c=65)
                sfv = SF[:, 65 * off:65 * (off + cnt_)].rearrange("p (h c) -> p h c", c=65)
                p.V(lambda e, view=view, sfv=sfv: e.tensor_tensor(sfv, sfv, view, op=ALU.add),
                    waits=[("tT", tS)] if g == 0 else ())
                p.V(lambda e, sfv=sfv, off=off, cnt_=cnt_, c=c: e.tensor_tensor(
                    sfv, sfv, bc_last(LAM[0:64, c, off:off + cnt_], 65), op=ALU.mult),
                    inc=(g == 2), drain=(g == 2))
            vsf = p.cnt["tV"]
            p.Sc(lambda e: e.copy(SBb[:], SF[:]), waits=[("tV", vsf)],
                 inc=True, drain=True)
            vSu = ("tS", p.cnt["tS"])
            vS = vSu
            bank = psX[c % 2]
            for j in range(NJ):
                w0 = [("tV", vO[c]), last_ao_copy]
                if gc_wait is not None:
                    w0.append(gc_wait)
                p.T(lambda e, j=j, bank=bank: e.transpose(
                    bank[:, 128 * j:128 * j + 128], AOC[:, 128 * j:128 * j + 128],
                    ident[:]),
                    waits=w0 if j == 0 else (),
                    inc=(j == NJ - 1))
            last_aot_tick = p.cnt["tT"]
            p.Sc(lambda e, c=c, bank=bank: e.copy(
                AOT[:, :, 128 * c:128 * c + 128],
                bank[:].rearrange("p (j c) -> p j c", c=128)),
                waits=[("tT", last_aot_tick)], inc=True, drain=True)
            last_ao_copy = ("tS", p.cnt["tS"])

        for c in range(NT):
            gsrc = [(QA, QTb, 0), (QA, QTb, 1), (KA, KTb, 0), (KA, KTb, 1)]
            gcopy = {}
            for g, (srcb, dstb, hi) in enumerate(gsrc):
                bank = psX[g % 2]
                w = []
                if g >= 2:
                    w.append(gcopy[g - 2])
                elif c > 0:
                    w.append(last_ao_copy)
                for hh in range(8):
                    hg = 8 * hi + hh
                    p.T(lambda e, hh=hh, hg=hg, c=c, bank=bank, srcb=srcb: e.transpose(
                        bank[0:64, 128 * hh:128 * hh + 128],
                        srcb[:, c, 64 * hg:64 * hg + 64], ident[:]),
                        waits=w if hh == 0 else (), inc=(hh == 7))
                ttr = p.cnt["tT"]
                p.Sc(lambda e, dstb=dstb, hi=hi, bank=bank: e.copy(
                    dstb[:, 8 * hi:8 * hi + 8, :],
                    bank[0:64, :].rearrange("p (h c) -> p h c", c=128)),
                    waits=[("tT", ttr)], inc=True, drain=True)
                gcopy[g] = ("tS", p.cnt["tS"])
            if c > 0:
                emit_tail(c - 1, gc_wait=gcopy[2 + ((c - 1) % 2)])
            vmask = {}
            tnum = 0

            def emit_mt(h):
                pb = psR[h % 2]
                if h >= 2:
                    w = [("tV", vmask[h - 2])]
                else:
                    w = [gcopy[2] if h < 8 else gcopy[3]]
                tmt = p.T(lambda e, h=h, pb=pb: e.matmul(
                    pb[:, 0:128], KTb[:, h, :], QTb[:, h, :], start=True, stop=True),
                    waits=w + [gcopy[3] if h >= 8 else gcopy[2]], inc=True)
                vmask[h] = p.V(lambda e, h=h, pb=pb: e.tensor_tensor(
                    MTS[:, h % 2, :], pb[:, 0:128], triub[:], op=ALU.mult),
                    waits=[("tT", tmt)], inc=True, drain=True)

            def emit_num(h):
                pb = psR[h % 2]
                bank_first = h in (0, 7, 14)
                w2 = [("tV", vmask[h]), vSu, vS]
                p.T(lambda e, h=h, c=c, bf=bank_first: e.matmul(
                    psN[:, cn(h):cn(h) + 65], MTS[:, h % 2, :],
                    VPA[:, c, 65 * h:65 * h + 65], start=bf, stop=False),
                    waits=w2)
                p.T(lambda e, h=h: e.matmul(
                    psN[:, cn(h):cn(h) + 65], QTb[:, h, :], SBb[:, 65 * h:65 * h + 65],
                    start=False, stop=True), inc=(h == H - 1))

            for h in range(H):
                emit_mt(h)
                if h >= 1:
                    emit_num(h - 1)
            emit_num(H - 1)
            tnum = p.cnt["tT"]
            for g, off, cnt_ in groups:
                view = psN[:, 512 * g:512 * g + 65 * cnt_].rearrange("p (h c) -> p h c", c=65)
                den = sq_last(view[:, :, 64:65])
                p.V(lambda e, den=den, off=off, cnt_=cnt_: e.tensor_scalar(
                    DENS[:, off:off + cnt_], den, 1e-6, None, op0=ALU.add),
                    waits=[("tT", tnum)], drain=True)
            p.V(lambda e: e.reciprocal(RECS[:], DENS[:]), inc=True, drain=True)
            vrec = p.cnt["tV"]
            aoc_v = AOC[:].rearrange("p (h c) -> p h c", c=64)
            for g, off, cnt_ in groups:
                view = psN[:, 512 * g:512 * g + 65 * cnt_].rearrange("p (h c) -> p h c", c=65)
                p.V(lambda e, view=view, off=off, cnt_=cnt_, aoc_v=aoc_v: e.tensor_tensor(
                    aoc_v[:, off:off + cnt_, :], view[:, :, 0:64],
                    bc_last(RECS[:, off:off + cnt_], 64), op=ALU.mult),
                    waits=[("tV", vrec), ("tT", last_aot_tick)],
                    inc=(g == 2), drain=(g == 2))
            vO[c] = p.cnt["tV"]
        emit_tail(NT - 1)
        p.barrier()

        emit_wo(wvw["wo_a"], x_v, add_bo=False, st1=wo_st1)

    def emit_wo(wo_view, xold_view, add_bo, st1=None):
        if st1 is None:
            wslab(0, wo_view, 0)
            st1 = wslab(1, wo_view, 1)
        ld_tick = {}
        ev_tick = {}
        for t in range(4):
            p.Y(lambda e, t=t: e.dma_start(XOLD[:, t % 4, :], xold_view[:, t, :]))
            ld_tick[t] = p.Yf()
        for t in range(NT):
            for half in range(2):
                pb = psR[(2 * t + half) % 2]
                w = [("dG", st1)]
                k_prev = (t, half - 2) if half >= 2 else (t - 1, half)
                if k_prev in ev_tick:
                    w.append(("tV", ev_tick[k_prev]))
                for j in range(NJ):
                    p.T(lambda e, t=t, j=j, pb=pb, half=half: e.matmul(
                        pb[:], AOT[:, j, 128 * t:128 * t + 128], WS[half][:, j, :],
                        start=(j == 0), stop=(j == NJ - 1)),
                        waits=w if j == 0 else (), inc=(j == NJ - 1))
                tmm = p.cnt["tT"]
                wv_ = [("tT", tmm), ("dY", ld_tick[t])]
                if add_bo:
                    p.V(lambda e, half=half, pb=pb: e.tensor_tensor(
                        TMPF[:], pb[:], BOBC[:, 512 * half:512 * half + 512], op=ALU.add),
                        waits=wv_, inc=True)
                    p.V(lambda e, t=t, half=half: e.tensor_tensor(
                        xtile(t)[:, 512 * half:512 * half + 512], TMPF[:],
                        XOLD[:, t % 4, 512 * half:512 * half + 512],
                        op=ALU.add), inc=True)
                else:
                    p.V(lambda e, t=t, half=half, pb=pb: e.tensor_tensor(
                        xtile(t)[:, 512 * half:512 * half + 512], pb[:],
                        XOLD[:, t % 4, 512 * half:512 * half + 512], op=ALU.add),
                        waits=wv_, inc=True)
                ev_tick[(t, half)] = p.cnt["tV"]
                if half == 1:
                    emit_ln_stats(t)
            if t + 4 < NT:
                p.Y(lambda e, t=t: e.dma_start(XOLD[:, t % 4, :], xold_view[:, t + 4, :]),
                    waits=[("tV", ev_tick[(t, 1)])])
                ld_tick[t + 4] = p.Yf()
        p.barrier()

    spill_ticks = [0, 0]

    def emit_ffn(spill_x=False):
        wslab(0, wvw["wr_t"], 0)
        st_r = wslab(1, wvw["wr_t"], 1)
        emit_ln(skip_stats=True)
        def emit_mix(mubuf):
            for j in range(NJ):
                p.V(lambda e, j=j: e.tensor_sub(TMPD[:, 1:S], XNT[:, j, 1:S],
                                                XNT[:, j, 0:S - 1]))
                p.V(lambda e, j=j: e.tensor_copy(TMPD[:, 0:1], XNT[:, j, 0:1]))
                p.V(lambda e, j=j, mubuf=mubuf: e.scalar_tensor_tensor(
                    XRB[:, j, 1:S], TMPD[:, 1:S], mubuf[:, j:j + 1],
                    XNT[:, j, 0:S - 1], op0=ALU.mult, op1=ALU.add))
                p.V(lambda e, j=j, mubuf=mubuf: e.tensor_scalar_mul(
                    XRB[:, j, 0:1], XNT[:, j, 0:1], mubuf[:, j:j + 1]),
                    inc=(j == NJ - 1), drain=(j == NJ - 1))
            return p.cnt["tV"]

        vmix = emit_mix(mur)
        st = st_r
        srg = {}
        stk = {}
        for half in range(2):
            for t in range(NT):
                pb = psR[t % 2]
                idx = NT * half + t
                w = [("dG", st), ("tV", vmix)]
                if idx >= 2 and (idx - 2) in srg:
                    w.append(("tS", srg[idx - 2]))
                for j in range(NJ):
                    p.T(lambda e, t=t, j=j, pb=pb, half=half: e.matmul(
                        pb[:], XRB[:, j, 128 * t:128 * t + 128], WS[half][:, j, :],
                        start=(j == 0), stop=(j == NJ - 1)),
                        waits=w if j == 0 else (), inc=(j == NJ - 1))
                tmm = p.cnt["tT"]
                srg[idx] = p.Sc(lambda e, t=t, half=half, pb=pb: e.activation(
                    RG[:, t, 512 * half:512 * half + 512], pb[:], AF.Sigmoid),
                    waits=[("tT", tmm)], inc=True)
            stk[half] = wslab(half, wvw["wk_t"], half, waits=[("tT", p.cnt["tT"])])
        p.barrier()
        vmix = emit_mix(muk)
        vev = {}
        stv = {}
        for half in range(2):
            for jo in range(4):
                jog = 4 * half + jo
                for sgrp in range(4):
                    pb = psR[(4 * jo + sgrp) % 2]
                    w = [("dG", stk[half]), ("tV", vmix)]
                    kprev = (jog, sgrp)
                    idx = 4 * jog + sgrp
                    if idx >= 2 and (idx - 2) in vev:
                        w.append(("tV", vev[idx - 2]))
                    for j in range(NJ):
                        p.T(lambda e, j=j, jo=jo, sgrp=sgrp, pb=pb, half=half: e.matmul(
                            pb[:], WS[half][:, j, 128 * jo:128 * jo + 128],
                            XRB[:, j, 512 * sgrp:512 * sgrp + 512],
                            start=(j == 0), stop=(j == NJ - 1)),
                            waits=w if j == 0 else (), inc=(j == NJ - 1))
                    tmm = p.cnt["tT"]
                    bK = TMPF if idx % 2 == 0 else TMPE
                    p.Sc(lambda e, pb=pb, bK=bK: e.activation(bK[:], pb[:], AF.Relu),
                         waits=w + [("tT", tmm)], inc=True)
                    srelu = p.cnt["tS"]
                    p.V(lambda e, jog=jog, sgrp=sgrp, bK=bK: e.tensor_tensor(
                        KAT[:, jog, 512 * sgrp:512 * sgrp + 512], bK[:], bK[:],
                        op=ALU.mult), waits=[("tS", srelu)], inc=True)
                    vev[idx] = p.cnt["tV"]
            stv[half] = wslab(half, wvw["wv_t"], half, waits=[("tT", p.cnt["tT"])])
        p.barrier()
        ev = {}
        for half in range(2):
            for t in range(NT):
                pb = psR[t % 2]
                w = [("dG", stv[half])]
                idx = NT * half + t
                if idx >= 2 and (idx - 2) in ev:
                    w.append(("tV", ev[idx - 2]))
                for j in range(NJ):
                    p.T(lambda e, t=t, j=j, pb=pb, half=half: e.matmul(
                        pb[:], KAT[:, j, 128 * t:128 * t + 128], WS[half][:, j, :],
                        start=(j == 0), stop=(j == NJ - 1)),
                        waits=w if j == 0 else (), inc=(j == NJ - 1))
                tmm = p.cnt["tT"]
                p.V(lambda e, t=t, half=half, pb=pb: e.tensor_tensor(
                    TMPF[:], pb[:], RG[:, t, 512 * half:512 * half + 512], op=ALU.mult),
                    waits=[("tT", tmm)], inc=True)
                p.V(lambda e, t=t, half=half: e.tensor_tensor(
                    xtile(t)[:, 512 * half:512 * half + 512],
                    xtile(t)[:, 512 * half:512 * half + 512], TMPF[:], op=ALU.add),
                    inc=True, drain=(spill_x and half == 1))
                ev[idx] = p.cnt["tV"]
                if spill_x and half == 1:
                    p.Y(lambda e, t=t: e.dma_start(xs_v[:, t, :], xtile(t)),
                        waits=[("tV", ev[idx])])
                    emit_ln_stats(t)
                    if t == 7:
                        spill_ticks[0] = p.Yf()
                    elif t == NT - 1:
                        spill_ticks[1] = p.Yf()
        p.barrier()

    def emit_attn_b():
        first_slabs = (wslab(0, wvw["wq_b"], 0), wslab(1, wvw["wk_b"], 0))
        ln_copy = emit_ln(end_barrier=False, skip_stats=True)
        tbl_t = 0
        for dst, hi, lo in ((SINR, "sinq_hi", "sinq_lo"), (COSR, "cosq_hi", "cosq_lo"),
                            (PHIR, "phi_hi", "phi_lo")):
            p.G(lambda e, hi=hi: e.dma_start(TBSTG[:, 0, :], d[hi]),
                waits=[("tV", tbl_t)])
            p.G(lambda e, lo=lo: e.dma_start(TBSTG[:, 1, :], d[lo]))
            t1 = p.Gf()
            p.V(lambda e: e.tensor_add(TBSTG[:, 0, :], TBSTG[:, 0, :], TBSTG[:, 1, :]),
                waits=[("dG", t1)], inc=True)
            tbl_t = p.V(lambda e, dst=dst: e.tensor_copy(dst[:], TBSTG[:, 0, :]), inc=True)
        tbl_done = ("tV", tbl_t)
        spill07, spill = spill_ticks[0], spill_ticks[1]

        last_sfix = [0]
        proj_slabs = {0: None, 1: None}
        wo_st1 = [None]
        odd_slot_free = {0: 0, 1: 0}
        proj_slabs[0] = first_slabs
        av_free = {0: [('tV', 0)], 1: [('tV', 0)]}
        bank_last_exp = {0: 0, 1: 0}
        last_e3 = [0]
        last_tb = [0]
        rec_done = [0]
        evac_done = {0: 0, 1: 0}
        pending_tail = [None]
        for half in range(2):
            p.V(lambda e: e.memset(VPB[:], 1.0), waits=[("dY", spill)], inc=True)
            stq, stkk = proj_slabs[half]
            pev = {}
            pidx = 0
            for which in range(2):
                stw = stq if which == 0 else stkk
                wsl = which
                for jc in range(4):
                    jcg = 4 * half + jc
                    for qq in range(4):
                        pb = psR[pidx % 2]
                        w = [("dG", stw), ("dY", spill07)]
                        if half == 0:
                            w.append(ln_copy[4 * qq + 3])
                        if pidx >= 2:
                            w.append(("tV", pev[pidx - 2]))
                        for j in range(NJ):
                            p.T(lambda e, j=j, jc=jc, qq=qq, pb=pb, wsl=wsl: e.matmul(
                                pb[:], WS[wsl][:, j, 128 * jc:128 * jc + 128],
                                XNT[:, j, 512 * qq:512 * qq + 512],
                                start=(j == 0), stop=(j == NJ - 1)),
                                waits=w if j == 0 else (), inc=(j == NJ - 1))
                        tmm = p.cnt["tT"]
                        if which == 0:
                            p.V(lambda e, jc=jc, qq=qq, jcg=jcg, pb=pb: e.tensor_scalar_add(
                                QUT[:, jc, 512 * qq:512 * qq + 512], pb[:],
                                u8b[:, jcg:jcg + 1]), waits=[("tT", tmm)], inc=True)
                        else:
                            p.V(lambda e, jc=jc, qq=qq, pb=pb: e.tensor_copy(
                                KTB[:, jc, 512 * qq:512 * qq + 512], pb[:]),
                                waits=[("tT", tmm)], inc=True)
                        pev[pidx] = p.cnt["tV"]
                        pidx += 1
                if which == 0:
                    stvv = wslab(0, wvw["wv_b"], half, waits=[("tT", p.cnt["tT"])])
            for t in range(NT):
                pb = psR[t % 2]
                w = [("dG", stvv)]
                if t >= 2:
                    w.append(("tV", pev[("v", t - 2)]))
                else:
                    w.append(("tV", pev[pidx - 1]))
                for j in range(NJ):
                    p.T(lambda e, t=t, j=j, pb=pb: e.matmul(
                        pb[:], XNT[:, j, 128 * t:128 * t + 128], WS[0][:, j, :],
                        start=(j == 0), stop=(j == NJ - 1)),
                        waits=w if j == 0 else (), inc=(j == NJ - 1))
                tmm = p.cnt["tT"]
                vpb_dst = VPB[:, t, :].rearrange("p (h c) -> p h c", c=65)[:, :, 0:64]
                src = pb[:].rearrange("p (h c) -> p h c", c=64)
                p.V(lambda e, vpb_dst=vpb_dst, src=src: e.tensor_copy(vpb_dst, src),
                    waits=[("tT", tmm)], inc=True, drain=True)
                pev[("v", t)] = p.cnt["tV"]
            if half == 0:
                proj_slabs[1] = (wslab(0, wvw["wq_b"], 1, waits=[("tT", p.cnt["tT"])]),
                                 wslab(1, wvw["wk_b"], 1, waits=[("tT", p.cnt["tT"])]))
            else:
                wslab(0, wvw["wo_b"], 0, waits=[("tT", p.cnt["tT"])])
                wo_st1[0] = wslab(1, wvw["wo_b"], 1)
            p.barrier()
            if half == 0:
                p.V(lambda e: e.memset(V2A[:, S - 1:S], 0.0))
                p.V(lambda e: e.memset(V2B[:, S - 1:S], 0.0))

            for jc in range(4):
                jcg = 4 * half + jc
                Ub, V2b = (UTA, V2A) if jc % 2 == 0 else (UTB, V2B)
                vq = p.V(lambda e, jc=jc, jcg=jcg: e.tensor_scalar_add(
                    QVT[:], QUT[:, jc, :], vmu[:, jcg:jcg + 1]), inc=True)
                for blki, (dst0, src0) in enumerate(((0, 32), (32, 0), (64, 96), (96, 64))):
                    p.Y(lambda e, dst0=dst0, src0=src0: e.dma_start(
                        SWP[dst0:dst0 + 32, :], QVT[src0:src0 + 32, :]),
                        waits=[("tV", vq)] if blki == 0 else ())
                sw = p.Yf()
                p.V(lambda e, Ub=Ub: e.tensor_tensor(Ub[:], QVT[:], SINR[:], op=ALU.mult),
                    waits=[("tV", vq)] + ([tbl_done] if half == 0 and jc == 0 else []))
                p.V(lambda e: e.scalar_tensor_tensor(QVT[:], SWP[:], sgn[:], COSR[:],
                                                     op0=ALU.mult, op1=ALU.mult),
                    waits=[("dY", sw)])
                vu = p.V(lambda e, Ub=Ub: e.tensor_tensor(Ub[:], Ub[:], QVT[:], op=ALU.add),
                         inc=True)
                for blki, (dst0, src0) in enumerate(((0, 32), (32, 0), (64, 96), (96, 64))):
                    p.Y(lambda e, dst0=dst0, src0=src0, Ub=Ub: e.dma_start(
                        SWP[dst0:dst0 + 32, :], Ub[src0:src0 + 32, :]),
                        waits=[("tV", vu)] if blki == 0 else ())
                sw = p.Yf()
                p.V(lambda e, Ub=Ub, V2b=V2b: e.tensor_scalar_mul(
                    V2b[:, 0:S - 1], Ub[:, 1:S], csv[:]), waits=[("tV", vu)])
                p.V(lambda e: e.tensor_scalar_mul(QVT[:, 0:S - 1], SWP[:, 1:S], ssv[:]),
                    waits=[("dY", sw)])
                vu2 = p.V(lambda e, V2b=V2b: e.tensor_add(
                    V2b[:, 0:S - 1], V2b[:, 0:S - 1], QVT[:, 0:S - 1]),
                    inc=True, drain=True)

                for qc in range(4):
                    exp_t = {}
                    av_t = {}
                    fix_v = {}
                    e3_t = {}
                    sfix_t = {}
                    d2_mm_t = {}
                    d3_mm_t = {}

                    def emit_scores(kc, e2):
                        po = 64 * e2
                        bidx = kc % 2
                        pbt = (psR[0], psR[1], psXA, psXB)[2 * bidx + e2]
                        pb = pbt[:]
                        lo_, hi_ = 4 * qc, 4 * qc + 4
                        band = lo_ <= kc <= hi_
                        w = [("tV", vu2), ("tS", bank_last_exp[bidx])]
                        p.T(lambda e, kc=kc, pb=pb, po=po, jc=jc, qc=qc: e.matmul(
                            pb, KTB[po:po + 64, jc, 128 * kc:128 * kc + 128],
                            QUT[po:po + 64, jc, 512 * qc:512 * qc + 512],
                            start=True, stop=False), waits=w)
                        sub_d = kc - 4 * qc if band else -1
                        sub_s = sub_d - 1
                        if not band:
                            rel = Ub if kc < lo_ else V2b
                            p.T(lambda e, kc=kc, pb=pb, po=po, rel=rel, qc=qc: e.matmul(
                                pb, PHIR[po:po + 64, 128 * kc:128 * kc + 128],
                                rel[po:po + 64, 512 * qc:512 * qc + 512],
                                start=False, stop=True), inc=True)
                        else:
                            for s_ in range(4):
                                qg = 4 * qc + s_
                                if s_ == sub_d:
                                    rel = Ub
                                elif s_ == sub_s:
                                    rel = V2b
                                elif qg > kc:
                                    rel = Ub
                                else:
                                    rel = V2b
                                sub = pbt[:, 128 * s_:128 * s_ + 128]
                                p.T(lambda e, kc=kc, qg=qg, s_=s_, sub=sub, po=po, rel=rel:
                                    e.matmul(
                                        sub,
                                        PHIR[po:po + 64, 128 * kc:128 * kc + 128],
                                        rel[po:po + 64, 128 * qg:128 * qg + 128],
                                        start=False, stop=(s_ == 3)),
                                    inc=(s_ == 3))
                            if 0 <= sub_d < 4:
                                d2c = 1024 + 128 * e2
                                p.T(lambda e, kc=kc, po=po, d2c=d2c, jc=jc: e.matmul(
                                    psN[:, d2c:d2c + 128],
                                    KTB[po:po + 64, jc, 128 * kc:128 * kc + 128],
                                    QUT[po:po + 64, jc, 128 * kc:128 * kc + 128],
                                    start=True, stop=False),
                                    waits=[("tS", last_sfix[0])] if e2 == 0 else ())
                                p.T(lambda e, kc=kc, po=po, V2b=V2b, d2c=d2c: e.matmul(
                                    psN[:, d2c:d2c + 128],
                                    PHIR[po:po + 64, 128 * kc:128 * kc + 128],
                                    V2b[po:po + 64, 128 * kc:128 * kc + 128],
                                    start=False, stop=True), inc=(e2 == 1))
                                if e2 == 1:
                                    d2_mm_t[kc] = p.cnt["tT"]
                            if 0 <= sub_d <= 4:
                                if sub_d == 0:
                                    q0b, nq, c0 = kc, 1, 256 * e2 + 128
                                elif sub_d == 4:
                                    q0b, nq, c0 = kc - 1, 1, 256 * e2
                                else:
                                    q0b, nq, c0 = kc - 1, 2, 256 * e2
                                p.T(lambda e, kc=kc, po=po, jc=jc, q0b=q0b, nq=nq, c0=c0:
                                    e.matmul(
                                        psR[2][:, c0:c0 + 128 * nq],
                                        KTB[po:po + 64, jc, 128 * kc:128 * kc + 128],
                                        QUT[po:po + 64, jc, 128 * q0b:128 * (q0b + nq)],
                                        start=True, stop=True),
                                    waits=[("tS", last_e3[0])] if e2 == 0 else (),
                                    inc=(e2 == 1))
                                if e2 == 1:
                                    d3_mm_t[kc] = p.cnt["tT"]
                        return sub_d, sub_s, p.cnt["tT"]

                    def emit_band_exps(kc, sub_d):
                        if not (0 <= sub_d <= 4):
                            return
                        if sub_d < 4:
                            p.Sc(lambda e: e.activation(
                                E2S[:, 0:2, :].rearrange("p a c -> p (a c)"),
                                psN[:, 1024:1280], AF.Exp, scale=0.125),
                                waits=[("tT", d2_mm_t[kc])], inc=True)
                            last_sfix[0] = p.cnt["tS"]
                            sfix_t[(kc, 0)] = sfix_t[(kc, 1)] = p.cnt["tS"]
                        p.Sc(lambda e: e.activation(
                            E3S[:, 0:2, :].rearrange("p a c -> p (a c)"),
                            psR[2][:, 0:512], AF.Exp, scale=0.125),
                            waits=[("tT", d3_mm_t[kc])], inc=True)
                        last_e3[0] = p.cnt["tS"]
                        e3_t[(kc, 0)] = e3_t[(kc, 1)] = p.cnt["tS"]

                    def emit_pair_exp(kc, tsc):
                        bidx = kc % 2
                        slot0 = (2 * kc) % 4
                        wexp = [("tT", tsc)]
                        if kc >= 2:
                            wexp.append(("tT", av_t[(kc - 2, 0)]))
                            wexp.append(("tT", av_t[(kc - 2, 1)]))
                        pw = (psRR, psXX)[bidx]
                        p.Sc(lambda e, pw=pw, slot0=slot0: e.activation(
                            ATT[:, slot0:slot0 + 2, :].rearrange("p s c -> p (s c)"),
                            pw[:], AF.Exp, scale=0.125),
                            waits=wexp, inc=True)
                        exp_t[(kc, 0)] = p.cnt["tS"]
                        exp_t[(kc, 1)] = p.cnt["tS"]
                        bank_last_exp[bidx] = p.cnt["tS"]

                    def emit_fixups(kc, e2, sub_d, sub_s):
                        slot = (2 * kc + e2) % 4
                        vfix = 0
                        if 0 <= sub_d < 4:
                            sfix = sfix_t[(kc, e2)]
                            p.V(lambda e, slot=slot, sub_d=sub_d, e2=e2: e.select(
                                ATT[:, slot, 128 * sub_d:128 * sub_d + 128], msel[:],
                                ATT[:, slot, 128 * sub_d:128 * sub_d + 128], E2S[:, e2, :]),
                                waits=[("tS", sfix), ("tS", exp_t[(kc, e2)]),
                                       ("tS", e3_t[(kc, e2)])])
                            p.V(lambda e, slot=slot, sub_d=sub_d, e2=e2: e.select(
                                ATT[:, slot, 128 * sub_d:128 * sub_d + 128], mlinv[:],
                                ATT[:, slot, 128 * sub_d:128 * sub_d + 128],
                                E3S[:, e2, 128:256]), inc=True, drain=True)
                            vfix = p.cnt["tV"]
                        if 0 <= sub_s < 4:
                            p.V(lambda e, slot=slot, sub_s=sub_s, e2=e2: e.select(
                                ATT[:, slot, 128 * sub_s:128 * sub_s + 128], msupv[:],
                                ATT[:, slot, 128 * sub_s:128 * sub_s + 128],
                                E3S[:, e2, 0:128]),
                                waits=[("tS", exp_t[(kc, e2)]), ("tS", e3_t[(kc, e2)])],
                                inc=True, drain=True)
                            vfix = p.cnt["tV"]
                        fix_v[(kc, e2)] = vfix

                    def emit_av(kc, e2):
                        hl = 2 * jc + e2
                        wav = [("tS", exp_t[(kc, e2)])]
                        if fix_v[(kc, e2)]:
                            wav.append(("tV", fix_v[(kc, e2)]))
                        if kc == 0:
                            wav.extend(av_free[e2])
                        slot = (2 * kc + e2) % 4
                        p.T(lambda e, kc=kc, hl=hl, slot=slot, e2=e2: e.matmul(
                            psN[0:65, 512 * e2:512 * e2 + 512],
                            VPB[:, kc, 65 * hl:65 * hl + 65],
                            ATT[:, slot, :], start=(kc == 0), stop=(kc == NT - 1)),
                            inc=True, waits=wav)
                        av_t[(kc, e2)] = p.cnt["tT"]

                    for kc in range(NT):
                        if kc == 2 and pending_tail[0] is not None:
                            pending_tail[0]()
                            pending_tail[0] = None
                        d0, s0_, _ = emit_scores(kc, 0)
                        d1, s1_, tsc = emit_scores(kc, 1)
                        if kc >= 2:
                            emit_av(kc - 2, 0)
                            emit_av(kc - 2, 1)
                        emit_band_exps(kc, d0)
                        emit_pair_exp(kc, tsc)
                        emit_fixups(kc, 0, d0, s0_)
                        emit_fixups(kc, 1, d1, s1_)
                        if half == 0 and jc == 0 and qc == 0 and kc == 0:
                            dump("ATT00", ATT[:], BF16)
                            dump("UT00", Ub[:], BF16)
                            dump("V200", V2b[:], BF16)
                            dump("QUT0", QUT[:], BF16)
                            dump("KTB0", KTB[:], BF16)
                            dump("PHIR0", PHIR[:], BF16)
                        if half == 0 and jc == 0 and qc == 0 and kc == 8:
                            dump("ATT08", ATT[:], BF16)
                    for kc in (NT - 2, NT - 1):
                        emit_av(kc, 0)
                        emit_av(kc, 1)
                    p.Sc(lambda e: e.activation(RSCR[64:65, :], psN[64:65, 0:512],
                                                AF.Ln),
                         waits=[("tT", av_t[(NT - 1, 0)])])
                    p.Sc(lambda e: e.activation(TMPE[64:65, :], RSCR[64:65, :],
                                                AF.Exp, scale=-1.0),
                         inc=True)
                    srec0 = p.cnt["tS"]
                    p.Sc(lambda e: e.activation(RSCR[64:65, :], psN[64:65, 512:1024],
                                                AF.Ln),
                         waits=[("tT", av_t[(NT - 1, 1)])])
                    p.Sc(lambda e: e.activation(TMPF[64:65, :], RSCR[64:65, :],
                                                AF.Exp, scale=-1.0),
                         inc=True)
                    srec1 = p.cnt["tS"]
                    p.V(lambda e: e.tensor_copy(TMPF[0:64, :], psN[0:64, 0:512]),
                        waits=[("tT", av_t[(NT - 1, 0)])], inc=True)
                    av_free[0] = [("tV", p.cnt["tV"]), ("tS", srec0)]
                    p.V(lambda e: e.tensor_copy(TMPE[0:64, :], psN[0:64, 512:1024]),
                        waits=[("tT", av_t[(NT - 1, 1)])], inc=True)
                    av_free[1] = [("tV", p.cnt["tV"]), ("tS", srec1)]
                    p.V(lambda e: e.tensor_copy(ODDS[64:65, 0, :], TMPE[64:65, :]),
                        waits=[("tS", srec0), ("tT", last_tb[0])])
                    p.V(lambda e: e.tensor_sub(ODDS[64:65, 1, :], TMPE[64:65, :],
                                               ODDS[64:65, 0, :]), inc=True, drain=True)
                    p.V(lambda e: e.tensor_copy(XNS[64:65, 0:512], TMPF[64:65, :]),
                        waits=[("tS", srec1)])
                    p.V(lambda e: e.tensor_sub(XNS[64:65, 512:1024], TMPF[64:65, :],
                                               XNS[64:65, 0:512]), inc=True, drain=True)
                    rec_done[0] = p.cnt["tV"]

                    def mk_tail(jcg=jcg, qc=qc):
                        def tail():
                            for e2 in range(2):
                                hi_ap = ODDS[64:65, 0, :] if e2 == 0 else XNS[64:65, 0:512]
                                lo_ap = ODDS[64:65, 1, :] if e2 == 0 else XNS[64:65, 512:1024]
                                avs = TMPF if e2 == 0 else TMPE
                                bc0 = 512 * e2
                                tb = p.T(lambda e, hi_ap=hi_ap, bc0=bc0: e.matmul(
                                    psN[0:64, bc0:bc0 + 512], allob[64:65, 0:64],
                                    hi_ap, start=True, stop=False),
                                    waits=[("tV", rec_done[0]), ("tV", evac_done[e2])])
                                tb = p.T(lambda e, lo_ap=lo_ap, bc0=bc0: e.matmul(
                                    psN[0:64, bc0:bc0 + 512], allob[64:65, 0:64],
                                    lo_ap, start=False, stop=True), inc=True)
                                last_tb[0] = tb
                                if e2 == 0:
                                    p.V(lambda e, jcg=jcg, qc=qc, avs=avs, bc0=bc0:
                                        e.tensor_tensor(
                                            AOT[0:64, jcg, 512 * qc:512 * qc + 512],
                                            avs[0:64, :], psN[0:64, bc0:bc0 + 512],
                                            op=ALU.mult),
                                        waits=[("tT", tb)], inc=True)
                                    evac_done[0] = p.cnt["tV"]
                                    av_free[0] = [("tV", p.cnt["tV"])]
                                else:
                                    oslot = qc % 2
                                    p.V(lambda e, oslot=oslot, avs=avs, bc0=bc0:
                                        e.tensor_tensor(
                                            ODDS[0:64, oslot, :], avs[0:64, :],
                                            psN[0:64, bc0:bc0 + 512], op=ALU.mult),
                                        waits=[("tT", tb), ("dY", odd_slot_free[oslot])],
                                        inc=True)
                                    evac_done[1] = p.cnt["tV"]
                                    av_free[1] = [("tV", p.cnt["tV"])]
                                    ev = p.cnt["tV"]
                                    p.Y(lambda e, oslot=oslot, jcg=jcg, qc=qc: e.dma_start(
                                        AOT[64:128, jcg, 512 * qc:512 * qc + 512],
                                        ODDS[0:64, oslot, :]),
                                        waits=[("tV", ev)])
                                    odd_slot_free[oslot] = p.Yf()
                        return tail

                    pending_tail[0] = mk_tail()
            if pending_tail[0] is not None:
                pending_tail[0]()
                pending_tail[0] = None
            p.barrier()
        dump("AOTB", AOT[:], BF16)
        p.G(lambda e: e.dma_start(XNS[0:1, :], d["bo"]))
        tbo = p.Gf()
        tb1 = p.T(lambda e: e.matmul(psR[2][:, 0:512], allob[0:1, :], XNS[0:1, 0:512],
                                     start=True, stop=True), waits=[("dG", tbo)])
        tb1 = p.T(lambda e: e.matmul(psN[:, 0:512], allob[0:1, :], XNS[0:1, 512:1024],
                                     start=True, stop=True), inc=True)
        p.V(lambda e: e.tensor_copy(BOBC[:, 0:512], psR[2][:, 0:512]),
            waits=[("tT", tb1)], inc=True)
        p.V(lambda e: e.tensor_copy(BOBC[:, 512:1024], psN[:, 0:512]), inc=True)
        emit_wo(wvw["wo_b"], xs_v, add_bo=True, st1=wo_st1[0])

    def emit_store():
        for t in range(NT):
            p.Y(lambda e, t=t: e.dma_start(out_v[:, t, :], xtile(t)))
        p.Yf()

    dump("X0", XT0[:], F32)
    dump("X1", XT1[:], F32)
    emit_ln(skip_stats=True)
    emit_attn_a()
    if cut == "attn_a":
        emit_store()
    else:
        emit_ffn(spill_x=True)
        if cut == "ffn1":
            emit_store()
        else:
            emit_attn_b()
            if cut == "attn_b":
                emit_store()
            else:
                emit_ffn()
                emit_store()

    from contextlib import ExitStack
    with ExitStack() as cm:
        E = cm.enter_context
        p.sems = {k: E(nc.semaphore(k)) for k in ("tT", "tV", "tS", "dY", "dG", "dS")}
        blk = E(nc.Block())
        blk.tensor(lambda eng: [f(eng) for f in p.ops["T"]])
        blk.vector(lambda eng: [f(eng) for f in p.ops["V"]])
        blk.scalar(lambda eng: [f(eng) for f in p.ops["Sc"]])
        blk.sync(lambda eng: [f(eng) for f in p.ops["Y"]])
        blk.gpsimd(lambda eng: [f(eng) for f in p.ops["G"]])
    return nc


def prep_inputs(inp):
    c = make_consts()
    common = {
        "wq_a": inp["L0_attn_a_Wq"], "wk_a": inp["L0_attn_a_Wk"],
        "wv_a": inp["L0_attn_a_Wv"], "wo_a": inp["L0_attn_a_Wo"],
        "wa_a": inp["L0_attn_a_Wa"],
        "ba": np.asarray(inp["L0_attn_a_ba"], np.float32).reshape(1, H),
        "wr_t": np.ascontiguousarray(np.asarray(inp["L0_ffn_W_r"], np.float32).T),
        "wk_t": np.ascontiguousarray(np.asarray(inp["L0_ffn_W_k"], np.float32).T),
        "wv_t": np.ascontiguousarray(np.asarray(inp["L0_ffn_W_v"], np.float32).T),
        "mu_r8": _chunk8(np.asarray(inp["L0_ffn_mu_r"], np.float32)),
        "mu_k8": _chunk8(np.asarray(inp["L0_ffn_mu_k"], np.float32)),
        "wq_b": inp["L0_attn_b_Wq"], "wk_b": inp["L0_attn_b_Wk"],
        "wv_b": inp["L0_attn_b_Wv"], "wo_b": inp["L0_attn_b_Wo"],
        "u8": _chunk8(np.asarray(inp["L0_attn_b_u"], np.float32).reshape(-1)),
        "v8": _chunk8(np.asarray(inp["L0_attn_b_v"], np.float32).reshape(-1)),
        "bo": np.asarray(inp["L0_attn_b_bo"], np.float32).reshape(1, D),
    }
    common.update(c)
    common = {k: np.ascontiguousarray(v) for k, v in common.items()}
    xfull = np.asarray(inp["L0_attn_a_x"], np.float32)
    in_maps = [dict(common, x=np.ascontiguousarray(xfull[b])) for b in range(8)]
    return in_maps


def kernel(**inputs):
    in_maps = prep_inputs(inputs)
    nc = build_nc(CUT)
    res = run_bass_kernel_spmd(nc, in_maps, core_ids=list(range(8)))
    out = np.stack([np.asarray(r["out"]) for r in res.results], 0)
    return out.astype(np.float32)
